# version 10
# baseline (speedup 1.0000x reference)
import numpy as np
from contextlib import ExitStack

import concourse.bass as bass
import concourse.mybir as mybir
from concourse.bass_utils import run_bass_kernel_spmd

F32 = mybir.dt.float32
BF16 = mybir.dt.bfloat16
I32 = mybir.dt.int32
U8 = mybir.dt.uint8
AF = mybir.ActivationFunctionType
ALU = mybir.AluOpType
AX = mybir.AxisListType

D = 1024
NCORE = 8


class Lz:
    __slots__ = ("tok",)

    def __init__(self):
        self.tok = None


class Prog:
    ENGS = ("sy", "gp", "ve", "sc", "te")

    def __init__(self, nc):
        self.nc = nc
        self.q = {e: [] for e in self.ENGS}
        self.cnt = {}
        self.sems = {}

    def sem(self, name):
        h = self.nc.alloc_semaphore(name=name)
        self.sems[name] = h
        self.cnt[name] = 0

    def step(self, eng, fn, waits=(), inc=None, add=1):
        tok = None
        if inc is not None:
            self.cnt[inc] += add
            tok = (inc, self.cnt[inc])
        ws = [w for w in waits if w is not None]
        self.q[eng].append((ws, fn, inc, add))
        return tok

    def replay(self, name, eng):
        seen = {}
        for ws, fn, inc, add in self.q[name]:
            for w in ws:
                if isinstance(w, Lz):
                    assert w.tok is not None, "unresolved lazy token"
                    s, v = w.tok
                else:
                    s, v = w
                if v > seen.get(s, 0):
                    eng.wait_ge(self.sems[s], v)
                    seen[s] = v
            ins = fn(eng)
            if inc is not None:
                ins.then_inc(self.sems[inc], add)

    def run(self):
        with self.nc.Block() as block:
            @block.sync
            def _(e):
                self.replay("sy", e)

            @block.gpsimd
            def _(e):
                self.replay("gp", e)

            @block.vector
            def _(e):
                self.replay("ve", e)

            @block.scalar
            def _(e):
                self.replay("sc", e)

            @block.tensor
            def _(e):
                self.replay("te", e)


def build(T):
    NT = T // 512
    NCH = T // 128
    nc = bass.Bass("TRN2", target_bir_lowering=False, debug=False,
                   dynamic_dma_scratch_size=512)

    xs = nc.dram_tensor("xs", [T, D], F32, kind="ExternalInput").ap()
    wqkv = nc.dram_tensor("wqkv", [D, 3 * D], F32, kind="ExternalInput").ap()
    bqkv = nc.dram_tensor("bqkv", [3 * D], F32, kind="ExternalInput").ap()
    wout = nc.dram_tensor("wout", [D, D], F32, kind="ExternalInput").ap()
    bout = nc.dram_tensor("bout", [D], F32, kind="ExternalInput").ap()
    wffn = nc.dram_tensor("wffn", [D, D], F32, kind="ExternalInput").ap()
    bffn = nc.dram_tensor("bffn", [D], F32, kind="ExternalInput").ap()
    bqc_in = nc.dram_tensor("bqc", [128, 8], F32, kind="ExternalInput").ap()
    idbf_in = nc.dram_tensor("idbf", [128, 128], BF16, kind="ExternalInput").ap()
    idf_in = nc.dram_tensor("idf", [128, 128], F32, kind="ExternalInput").ap()
    onescb_in = nc.dram_tensor("onescb", [128, 1], BF16, kind="ExternalInput").ap()
    onesrb_in = nc.dram_tensor("onesrb", [1, 128], BF16, kind="ExternalInput").ap()
    onescf_in = nc.dram_tensor("onescf", [128, 1], F32, kind="ExternalInput").ap()
    onesrf_in = nc.dram_tensor("onesrf", [1, 128], F32, kind="ExternalInput").ap()
    out = nc.dram_tensor("out", [T, D], F32, kind="ExternalOutput").ap()
    x1scr = nc.dram_tensor("x1scr", [T, D], F32).ap()
    rqscr = nc.dram_tensor("rqscr", [T], F32).ap()
    cc_kv_i = nc.dram_tensor("cc_kv_i", [8, 128], F32)
    cc_kv_o = nc.dram_tensor("cc_kv_o", [8, 128], F32)
    cc_mx_i = nc.dram_tensor("cc_mx_i", [1, 16], F32)
    cc_mx_o = nc.dram_tensor("cc_mx_o", [1, 16], F32)

    es = ExitStack()
    sbt = lambda n, s, d: es.enter_context(nc.sbuf_tensor(n, s, d))
    pst = lambda n, s, d: es.enter_context(nc.psum_tensor(n, s, d))

    Pq = [sbt(f"pq{i}", [128, T], BF16) for i in range(8)]
    arena = sbt("arena", [128, 49152], U8)
    abase = nc.lookup_mloc(arena).addr
    Wq = [nc.alloc_sbuf_tensor_at(f"wq{i}", [128, 3 * D], BF16, offset=abase + i * 6144)
          for i in range(8)]
    qw = [nc.alloc_sbuf_tensor_at(f"qwq{i}", [128, D], BF16, offset=abase + i * 2048)
          for i in range(8)]
    qwT = nc.alloc_sbuf_tensor_at("qwT", [128, 8 * D], BF16, offset=abase + 16384)
    tmpB = [nc.alloc_sbuf_tensor_at(f"tmpb{i}", [128, D], F32, offset=abase + 32768 + i * 4096)
            for i in range(2)]
    Wo = [sbt(f"wo{i}", [128, D], BF16) for i in range(8)]
    h1T = [sbt(f"h1t{i}", [128, 8 * 512], BF16) for i in range(2)]
    xch = [sbt(f"xch{i}", [128, D], F32) for i in range(3)]
    h1ch = [sbt(f"h1ch{i}", [128, D], BF16) for i in range(2)]
    kch = [sbt(f"kch{i}", [128, D], BF16) for i in range(2)]
    vch = [sbt(f"vch{i}", [128, D], BF16) for i in range(2)]
    kvp = [sbt(f"kvp{i}", [128, D], BF16) for i in range(2)]
    qsqs = [sbt(f"qsqs{i}", [128, 512], BF16) for i in range(2)]
    scf = [sbt(f"scf{i}", [128, D], F32) for i in range(2)]
    ubuf = [sbt("ubuf0", [128, 512], F32)]
    ibuf = [[sbt(f"ibuf{i}{h}", [128, 512], I32) for h in range(2)] for i in range(2)]
    qnch = [sbt(f"qnch{i}", [128, D], BF16) for i in range(2)]
    qiT = [sbt(f"qit{i}", [128, D], BF16) for i in range(2)]
    bkr = sbt("bkr", [128, D], F32)
    bvr = sbt("bvr", [128, D], F32)
    bor = sbt("bor", [128, D], F32)
    bqc = sbt("bqcs", [128, 8], F32)
    bfs = sbt("bfs", [1, D], F32)
    bprime = sbt("bprime", [1, D], BF16)
    idbf = sbt("idbfs", [128, 128], BF16)
    idf = sbt("idfs", [128, 128], F32)
    onescb = sbt("onescbs", [128, 1], BF16)
    onesrb = sbt("onesrbs", [1, 128], BF16)
    onescf = sbt("onescfs", [128, 1], F32)
    onesrf = sbt("onesrfs", [1, 128], F32)
    bnraw = [sbt(f"bnraw{i}", [128, 12], F32) for i in range(2)]
    mv = [sbt(f"mv{i}", [128, 2], F32) for i in range(2)]
    rcol = [sbt(f"rcol{i}", [128, 1], F32) for i in range(2)]
    nmcol = [sbt(f"nmcol{i}", [128, 1], F32) for i in range(2)]
    kss = [sbt(f"kss{i}", [128, 1], F32) for i in range(2)]
    rkcol = [sbt(f"rkcol{i}", [128, 1], F32) for i in range(2)]
    r2all = sbt("r2all", [128, NCH], F32)
    nm2all = sbt("nm2all", [128, NCH], F32)
    sc2all = sbt("sc2all", [128, NCH], F32)
    bc2all = sbt("bc2all", [128, NCH], F32)
    sqt = [sbt(f"sqt{i}", [128, 1], F32) for i in range(2)]
    ksq = [sbt(f"ksq{i}", [128, 1], F32) for i in range(2)]
    t1c = [sbt(f"t1c{i}", [128, 1], F32) for i in range(2)]
    t2c = [sbt(f"t2c{i}", [128, 1], F32) for i in range(2)]
    rqrows = sbt("rqrows", [32, 128], F32)
    rqcols = sbt("rqcols", [128, 32], F32)
    kvrows = sbt("kvrows", [8, 128], F32)
    kvcols = sbt("kvcols", [128, 8], F32)
    mxcols = sbt("mxcols", [128, NCH], F32)
    mxfin = sbt("mxfin", [128, 1], F32)
    mxrow = sbt("mxrow", [1, 128], F32)
    mxstrip = sbt("mxstrip", [1, 16], F32)
    wsums = sbt("wsums", [128, 8], F32)
    wsum1 = sbt("wsum1", [128, 1], F32)
    strips = sbt("strips", [1, 16], F32)
    thrc = sbt("thrc", [128, 1], F32)
    nthrc = sbt("nthrc", [128, 1], F32)
    invc = sbt("invc", [128, 1], F32)
    cc_c = sbt("ccc", [128, 1], F32)
    c_eps = sbt("ceps", [128, 1], F32)
    c_i127 = sbt("ci127", [1, 1], F32)
    c_invM = sbt("cinvm", [1, 1], F32)
    c_h = sbt("chalf", [1, 1], F32)

    pT = pst("p_tr", [128, 512], F32)
    pQ = [pst(f"p_q{i}", [128, 512], F32) for i in range(2)]
    pK = pst("p_k", [128, 1024], F32)
    pV = pst("p_v", [128, 1024], F32)
    pR = pst("p_r", [128, 512], F32)
    pT_bf = pT.bitcast(BF16)

    P = Prog(nc)
    for s in ("dx0", "dx1", "dx2", "dc", "dt0", "dt1", "dw0", "dw1", "dw2",
              "ds", "do0", "do1", "ccs", "pv", "psc", "pt", "pg", "dg"):
        P.sem(s)

    xslot_cons = [[], [], []]

    def load_x(idx, src_ap):
        s = idx % 3
        tok = P.step("sy", lambda e, a=src_ap, s=s: e.dma_start(out=xch[s][:], in_=a),
                     waits=list(xslot_cons[s]), inc=f"dx{s}", add=16)
        xslot_cons[s] = []
        return tok, s

    const_srcs = [
        (idbf, idbf_in), (idf, idf_in), (onescb, onescb_in), (onesrb, onesrb_in),
        (onescf, onescf_in), (onesrf, onesrf_in), (bqc, bqc_in),
    ]
    for dst, src in const_srcs:
        P.step("sy", lambda e, d=dst, s=src: e.dma_start(out=d[:], in_=s),
               inc="dc", add=16)
    P.step("sy", lambda e: e.dma_start(out=bkr[:], in_=bqkv[D:2 * D].partition_broadcast(128)),
           inc="dc", add=16)
    P.step("sy", lambda e: e.dma_start(out=bvr[:], in_=bqkv[2 * D:3 * D].partition_broadcast(128)),
           inc="dc", add=16)
    P.step("sy", lambda e: e.dma_start(out=bor[:], in_=bout.partition_broadcast(128)),
           inc="dc", add=16)
    tok_consts = P.step("sy", lambda e: e.dma_start(
        out=bfs[:], in_=bffn.rearrange("(a b) -> a b", a=1)), inc="dc", add=16)

    def f_msets(e):
        e.memset(c_eps[:], 1e-5)
        e.memset(c_i127[:], 1.0 / 127.0)
        e.memset(c_invM[:], 1.0 / float(D * D))
        e.memset(c_h[:], 0.5)
        return e.memset(mxstrip[:], 0.0)
    tok_msets = P.step("ve", f_msets, inc="pv")
    for g in ("ve", "sc", "te"):
        P.step(g, lambda e: e.nop(), waits=[tok_consts, tok_msets])

    preload_x = {}
    for c in range(3):
        preload_x[c] = load_x(c, xs[c * 128:(c + 1) * 128, :])

    stage_cons = [[], []]
    wf_abs_toks = []
    for dc in range(8):
        P.step("gp", lambda e, dc=dc: e.dma_start(
            out=Wq[dc][:, :], in_=wqkv[dc * 128:(dc + 1) * 128, :]),
            inc="dg", add=16)
    tok_wq_kv = ("dg", 16 * 16)
    tok_wq_all = ("dg", 16 * 16)
    for dc in range(8):
        P.step("gp", lambda e, dc=dc: e.dma_start(
            out=Wo[dc][:, :], in_=wout[dc * 128:(dc + 1) * 128, :]),
            inc="dg", add=16)
    tok_wo_all = ("dg", 16 * 16)

    h1T_cons = [[], []]
    h1_cons = [[], []]
    k_cons = [[], []]
    v_cons = [[], []]
    kvp_cons = [[], []]
    stat_cons = [[], []]
    kstat_cons = [[], []]
    pT_free = [None]
    pQ_free = [None, None]
    pK_free = [None]
    pV_free = [None]
    tok_qevac = {}
    tok_h1T_tile = [None] * NT
    tok_kvacc = None
    qsq_cons = [[], []]
    rq_dma_toks = []
    pend_kvacc = []
    kv_fin = [None]

    trev_toks = [None] * NCH

    def front_A(c):
        t, j = c // 4, c % 4
        s2 = c % 2
        hb = t % 2
        hc = c % 2
        if c < 3:
            tok_x, xsl = preload_x[c]
        else:
            tok_x, xsl = load_x(c, xs[c * 128:(c + 1) * 128, :])
        tok_bn0 = P.step("ve", lambda e, s2=s2, xsl=xsl: e.bn_stats(
            out=bnraw[s2][:, 0:6], in_=xch[xsl][:, 0:512]),
            waits=[tok_x] + list(stat_cons[s2]), inc="pv")
        tok_bn = P.step("ve", lambda e, s2=s2, xsl=xsl: e.bn_stats(
            out=bnraw[s2][:, 6:12], in_=xch[xsl][:, 512:1024]),
            waits=[tok_x], inc="pv")
        stat_cons[s2][:] = []
        xslot_cons[xsl] += [tok_bn0, tok_bn]
        tok_ag = P.step("ve", lambda e, s2=s2: e.bn_aggr(out=mv[s2][:], in_=bnraw[s2][:]),
                        waits=[tok_bn0, tok_bn], inc="pv")
        tok_sq = P.step("sc", lambda e, s2=s2: e.activation(
            out=sqt[s2][:], in_=mv[s2][:, 1:2], func=AF.Sqrt, bias=c_eps[:]),
            waits=[tok_ag], inc="psc")
        tok_r = P.step("ve", lambda e, s2=s2: e.reciprocal(
            out=rcol[s2][:], in_=sqt[s2][:]),
            waits=[tok_sq], inc="pv")
        tok_nm = P.step("ve", lambda e, s2=s2: e.scalar_tensor_tensor(
            out=nmcol[s2][:], in0=mv[s2][:, 0:1], scalar=-1.0, in1=rcol[s2][:],
            op0=ALU.mult, op1=ALU.mult),
            waits=[tok_r, tok_ag], inc="pv")
        tok_h1 = P.step("sc", lambda e, s2=s2, hc=hc, xsl=xsl: e.activation(
            out=h1ch[hc][:], in_=xch[xsl][:], func=AF.Identity,
            scale=rcol[s2][:], bias=nmcol[s2][:]),
            waits=[tok_nm, tok_r] + list(h1_cons[hc]), inc="psc")
        h1_cons[hc][:] = []
        xslot_cons[xsl].append(tok_h1)
        stat_cons[s2][:] = [tok_h1, tok_nm, tok_r]

        def f_tr(e, hc=hc):
            ins = None
            for dc in range(8):
                ins = e.transpose(pT_bf[:, dc * 128:(dc + 1) * 128],
                                  h1ch[hc][:, dc * 128:(dc + 1) * 128], idbf[:])
            return ins
        tok_tr = P.step("te", f_tr, waits=[tok_h1, pT_free[0]], inc="pt")

        def f_trev(e, hb=hb, j=j):
            dst = h1T[hb].ap().rearrange("p (dc t) -> p dc t", dc=8)[:, :, j * 128:(j + 1) * 128]
            return e.copy(out=dst, in_=pT_bf.ap().rearrange("p (dc t) -> p dc t", dc=8))
        tok_trev = P.step("sc", f_trev,
                          waits=[tok_tr] + (list(h1T_cons[hb]) if j == 0 else []),
                          inc="psc")
        if j == 0:
            h1T_cons[hb][:] = []
        pT_free[0] = tok_trev
        h1_cons[hc].append(tok_trev)
        if j == 3:
            tok_h1T_tile[t] = tok_trev
        trev_toks[c] = tok_trev

    def back_A(c):
        t, j = c // 4, c % 4
        s2 = c % 2
        hb = t % 2
        tok_trev = trev_toks[c]
        if len(pend_kvacc) >= 2:
            kv_fin[0] = pend_kvacc.pop(0)()

        def f_k(e, hb=hb, j=j):
            ins = None
            for dc in range(8):
                lhs = h1T[hb][:, dc * 512 + j * 128: dc * 512 + (j + 1) * 128]
                st, sp = dc == 0, dc == 7
                e.matmul(pK[:, 0:512], lhsT=lhs, rhs=Wq[dc][:, D:D + 512], start=st, stop=sp)
                ins = e.matmul(pK[:, 512:1024], lhsT=lhs, rhs=Wq[dc][:, D + 512:2 * D], start=st, stop=sp)
            return ins
        tok_kmm = P.step("te", f_k, waits=[tok_trev, tok_wq_kv, pK_free[0]], inc="pt")

        def f_v(e, hb=hb, j=j):
            ins = None
            for dc in range(8):
                lhs = h1T[hb][:, dc * 512 + j * 128: dc * 512 + (j + 1) * 128]
                st, sp = dc == 0, dc == 7
                e.matmul(pV[:, 0:512], lhsT=lhs, rhs=Wq[dc][:, 2 * D:2 * D + 512], start=st, stop=sp)
                ins = e.matmul(pV[:, 512:1024], lhsT=lhs, rhs=Wq[dc][:, 2 * D + 512:3 * D], start=st, stop=sp)
            return ins
        tok_vmm = P.step("te", f_v, waits=[pV_free[0]], inc="pt")
        h1T_cons[hb].append(tok_vmm)

        tok_ke = P.step("ve", lambda e, s2=s2: e.scalar_tensor_tensor(
            out=kch[s2][:], in0=pK.ap(), scalar=1.0, in1=bkr[:],
            op0=ALU.bypass, op1=ALU.add),
            waits=[tok_kmm, tok_consts] + list(k_cons[s2]), inc="pv")
        k_cons[s2][:] = []
        pK_free[0] = tok_ke
        tok_vev = P.step("ve", lambda e, s2=s2: e.scalar_tensor_tensor(
            out=vch[s2][:], in0=pV.ap(), scalar=1.0, in1=bvr[:],
            op0=ALU.bypass, op1=ALU.add),
            waits=[tok_vmm] + list(v_cons[s2]), inc="pv")
        v_cons[s2][:] = []
        pV_free[0] = tok_vev

        tok_ks = P.step("sc", lambda e, s2=s2: e.activation(
            out=kvp[s2][:], in_=kch[s2][:], func=AF.Square, accum_out=kss[s2][:]),
            waits=[tok_ke] + list(kvp_cons[s2]) + list(kstat_cons[s2]), inc="psc")
        kvp_cons[s2][:] = []
        tok_rk0 = P.step("sc", lambda e, s2=s2: e.activation(
            out=ksq[s2][:], in_=kss[s2][:], func=AF.Sqrt),
            waits=[tok_ks] + list(kstat_cons[s2]), inc="psc")
        tok_rk = P.step("ve", lambda e, s2=s2: e.reciprocal(
            out=rkcol[s2][:], in_=ksq[s2][:]),
            waits=[tok_rk0], inc="pv")
        tok_kvp = P.step("ve", lambda e, s2=s2: e.scalar_tensor_tensor(
            out=kvp[s2][:], in0=kch[s2][:], scalar=rkcol[s2][:], in1=vch[s2][:],
            op0=ALU.mult, op1=ALU.mult),
            waits=[tok_rk, tok_vev, tok_ks], inc="pv")
        k_cons[s2].append(tok_kvp)
        v_cons[s2].append(tok_kvp)
        kstat_cons[s2][:] = [tok_kvp]

        def q_kvacc(s2=s2, c=c, tok_kvp=tok_kvp):
            def f_kvacc(e):
                e.matmul(pR[0:1, :], lhsT=onescb[:], rhs=kvp[s2][:, 0:512],
                         start=(c == 0), stop=(c == NCH - 1))
                return e.matmul(pR[32:33, :], lhsT=onescb[:], rhs=kvp[s2][:, 512:1024],
                                start=(c == 0), stop=(c == NCH - 1))
            tok = P.step("te", f_kvacc, waits=[tok_kvp, tok_consts], inc="pt")
            kvp_cons[s2].append(tok)
            return tok
        pend_kvacc.append(q_kvacc)

        if j == 3:
            if t == NT - 1:
                while pend_kvacc:
                    kv_fin[0] = pend_kvacc.pop(0)()
            for ec in range(8):
                def f_q(e, t=t, ec=ec, hb=hb):
                    ins = None
                    for dc in range(8):
                        ins = e.matmul(pQ[ec % 2][:],
                                       lhsT=Wq[dc][:, ec * 128:(ec + 1) * 128],
                                       rhs=h1T[hb][:, dc * 512:(dc + 1) * 512],
                                       start=(dc == 0), stop=(dc == 7))
                    return ins
                tok_q = P.step("te", f_q,
                               waits=[tok_h1T_tile[t], tok_wq_all, pQ_free[ec % 2]],
                               inc="pt")
                h1T_cons[hb].append(tok_q)
                tok_ev = P.step("sc", lambda e, t=t, ec=ec: e.activation(
                    out=Pq[ec][:, t * 512:(t + 1) * 512], in_=pQ[ec % 2].ap(),
                    func=AF.Identity, bias=bqc[:, ec:ec + 1]),
                    waits=[tok_q, tok_consts], inc="psc")
                tok_qevac[(t, ec)] = tok_ev
                pQ_free[ec % 2] = tok_ev
            tok_sq_mm = None
            for ec in range(8):
                tok_sq = P.step("ve", lambda e, t=t, ec=ec: e.tensor_mul(
                    qsqs[ec % 2][:], Pq[ec][:, t * 512:(t + 1) * 512],
                    Pq[ec][:, t * 512:(t + 1) * 512]),
                    waits=[tok_qevac[(t, ec)]] + list(qsq_cons[ec % 2]), inc="pv")
                qsq_cons[ec % 2][:] = []
                tok_sq_mm = P.step("te", lambda e, ec=ec: e.matmul(
                    pR[64:65, :], lhsT=onescb[:], rhs=qsqs[ec % 2][:],
                    start=(ec == 0), stop=(ec == 7)),
                    waits=[tok_sq], inc="pt")
                qsq_cons[ec % 2].append(tok_sq_mm)
            tok_qse = P.step("sc", lambda e: e.copy(
                out=ubuf[0][0:1, :], in_=pR[64:65, :]),
                waits=[tok_sq_mm] + (rq_dma_toks[-1:] if t >= 1 else []), inc="psc")
            tok_rqw = P.step("sy", lambda e, t=t: e.dma_start(
                out=rqscr[t * 512:(t + 1) * 512].rearrange("(a b) -> a b", a=1),
                in_=ubuf[0][0:1, :]),
                waits=[tok_qse], inc="ds", add=16)
            rq_dma_toks.append(tok_rqw)

    def wf1_pair(oc):
        sl = oc % 2
        tok_l = P.step("sy" if sl == 0 else "sc",
            lambda e, oc=oc, sl=sl: e.dma_start(
                out=scf[sl][:], in_=wffn[oc * 128:(oc + 1) * 128, :]),
            waits=list(stage_cons[sl]), inc=f"dt{sl}", add=16)
        stage_cons[sl][:] = []
        tok_a = P.step("ve", lambda e, oc=oc, sl=sl: e.reduce_sum(
            out=wsums[:, oc:oc + 1], in_=scf[sl][:], axis=AX.X,
            apply_absolute_value=True),
            waits=[tok_l], inc="pv")
        stage_cons[sl].append(tok_a)
        wf_abs_toks.append(tok_a)

    for c in range(NCH):
        front_A(c)
        if c >= 2:
            back_A(c - 2)
        if NCH >= 32 and c >= NCH - 16 and c % 2 == 0:
            wf1_pair((c - (NCH - 16)) // 2)
    back_A(NCH - 2)
    back_A(NCH - 1)
    if NCH < 32:
        for oc in range(8):
            wf1_pair(oc)

    while pend_kvacc:
        kv_fin[0] = pend_kvacc.pop(0)()
    tok_kvacc = kv_fin[0]

    tok_ws1 = P.step("ve", lambda e: e.reduce_sum(out=wsum1[:], in_=wsums[:], axis=AX.X),
                     waits=[wf_abs_toks[-1]], inc="pv")
    tok_wsmm = P.step("te", lambda e: e.matmul(
        pQ[0][0:1, 4:5], lhsT=onescf[:], rhs=wsum1[:], start=True, stop=True),
        waits=[tok_ws1, pQ_free[0]], inc="pt")
    tok_wsc = P.step("sc", lambda e: e.activation(
        out=strips[0:1, 0:1], in_=pQ[0][0:1, 4:5], func=AF.Identity, scale=c_invM[:]),
        waits=[tok_wsmm, tok_msets], inc="psc")
    tok_thr = P.step("sc", lambda e: e.activation(
        out=strips[0:1, 1:2], in_=strips[0:1, 0:1], func=AF.Identity, scale=c_h[:]),
        waits=[tok_wsc], inc="psc")
    tok_thrmm = P.step("te", lambda e: e.matmul(
        pQ[0][:, 0:1], lhsT=onesrf[:], rhs=strips[0:1, 1:2], start=True, stop=True),
        waits=[tok_thr, pQ_free[0]], inc="pt")
    tok_thrc = P.step("sc", lambda e: e.copy(out=thrc[:], in_=pQ[0][:, 0:1]),
                      waits=[tok_thrmm], inc="psc")
    pQ_free[0] = tok_thrc
    tok_nthr = P.step("ve", lambda e: e.tensor_scalar(
        out=nthrc[:], in0=thrc[:], scalar1=-1.0, scalar2=None, op0=ALU.mult),
        waits=[tok_thrc], inc="pv")

    def f_kvev(e):
        e.copy(out=ubuf[0][32:33, :], in_=pR[0:1, :])
        return e.copy(out=ubuf[0][64:65, :], in_=pR[32:33, :])
    tok_kve = P.step("sc", f_kvev, waits=[tok_kvacc], inc="psc")
    P.step("sy", lambda e: e.dma_start(
        out=cc_kv_i.ap()[0:4, :], in_=ubuf[0][32:33, :]),
        waits=[tok_kve], inc="ds", add=16)
    tok_kvw = P.step("sy", lambda e: e.dma_start(
        out=cc_kv_i.ap()[4:8, :], in_=ubuf[0][64:65, :]),
        waits=[tok_kve], inc="ds", add=16)
    tok_cc1 = P.step("gp", lambda e: e.collective_compute(
        "AllReduce", ALU.add, replica_groups=[[0, 1, 2, 3], [4, 5, 6, 7]],
        ins=[cc_kv_i.ap().opt()], outs=[cc_kv_o.ap().opt()]),
        waits=[tok_kvw], inc="ccs")
    tok_kvr = P.step("sy", lambda e: e.dma_start(out=kvrows[:], in_=cc_kv_o.ap()),
                     waits=[tok_cc1], inc="ds", add=16)
    tok_kvtr = P.step("te", lambda e: e.transpose(pQ[1][:, 0:8], kvrows[:], idf[0:8, 0:8]),
                      waits=[tok_kvr, pQ_free[1]], inc="pt")
    tok_kvc = P.step("sc", lambda e: e.copy(out=kvcols[:], in_=pQ[1][:, 0:8]),
                     waits=[tok_kvtr], inc="psc")
    pQ_free[1] = tok_kvc

    def f_w1(e):
        ins = None
        for dc in range(8):
            ins = e.tensor_scalar(out=Wo[dc][:], in0=Wo[dc][:],
                                  scalar1=kvcols[:, dc:dc + 1], scalar2=None,
                                  op0=ALU.mult)
        return ins
    tok_w1 = P.step("ve", f_w1, waits=[tok_kvc, tok_wo_all], inc="pv")

    tok_rqr = P.step("sy", lambda e: e.dma_start(
        out=rqrows[0:NCH, :], in_=rqscr.rearrange("(a b) -> a b", b=128)),
        waits=[("ds", P.cnt["ds"])], inc="ds", add=16)
    tok_rqs = P.step("sc", lambda e: e.activation(
        out=rqrows[0:NCH, :], in_=rqrows[0:NCH, :], func=AF.Sqrt),
        waits=[tok_rqr], inc="psc")
    tok_rqv = P.step("ve", lambda e: e.reciprocal(
        out=rqrows[0:NCH, :], in_=rqrows[0:NCH, :]),
        waits=[tok_rqs], inc="pv")
    tok_rqt = P.step("te", lambda e: e.transpose(
        pQ[0][:, 0:NCH], rqrows[0:NCH, :], idf[0:NCH, 0:NCH]),
        waits=[tok_rqv, pQ_free[0]], inc="pt")
    tok_rqc = P.step("sc", lambda e: e.copy(out=rqcols[:, 0:NCH], in_=pQ[0][:, 0:NCH]),
                     waits=[tok_rqt], inc="psc")
    pQ_free[0] = tok_rqc

    qw_toks = []
    qn_guard = [[], []]
    for oc in range(8):
        sl = oc % 2
        tok_l = P.step("sy" if sl == 0 else "sc",
            lambda e, oc=oc, sl=sl: e.dma_start(
                out=scf[sl][:], in_=wffn[oc * 128:(oc + 1) * 128, :]),
            waits=list(stage_cons[sl]), inc=f"dt{sl}", add=16)
        stage_cons[sl] = []
        tok_ta = P.step("ve", lambda e, sl=sl: e.tensor_single_scalar(
            out=qnch[0][:], in_=scf[sl][:], scalar=thrc[:], op=ALU.is_gt),
            waits=[tok_l, tok_nthr] + list(qn_guard[0]), inc="pv")
        tok_tb2 = P.step("ve", lambda e, sl=sl: e.tensor_single_scalar(
            out=qnch[1][:], in_=scf[sl][:], scalar=nthrc[:], op=ALU.is_lt),
            waits=[tok_l, tok_nthr] + list(qn_guard[1]), inc="pv")
        tok_q = P.step("ve", lambda e, oc=oc: e.tensor_sub(
            qw[oc][:], qnch[0][:], qnch[1][:]),
            waits=[tok_ta, tok_tb2], inc="pv")
        qn_guard[0] = [tok_q]
        qn_guard[1] = [tok_q]
        stage_cons[sl].append(tok_q)
        qw_toks.append(tok_q)
    qwT_toks = []
    for oc in range(8):
        def f_qtr(e, oc=oc):
            ins = None
            for dc in range(8):
                ins = e.transpose(pT_bf[:, dc * 128:(dc + 1) * 128],
                                  qw[oc][:, dc * 128:(dc + 1) * 128], idbf[:])
            return ins
        tok_tr = P.step("te", f_qtr, waits=[qw_toks[oc], pT_free[0]], inc="pt")

        def f_qte(e, oc=oc):
            dst = qwT.ap().rearrange("p (dc o) -> p dc o", dc=8)[:, :, oc * 128:(oc + 1) * 128]
            return e.copy(out=dst, in_=pT_bf.ap().rearrange("p (dc t) -> p dc t", dc=8))
        tok_te = P.step("sc", f_qte, waits=[tok_tr], inc="psc")
        pT_free[0] = tok_te
        qwT_toks.append(tok_te)

    tmpB_cons = [[], []]
    lz_tmp = [Lz() for _ in range(NCH)]
    pA_free = [pK_free[0], pV_free[0]]
    tok_mx = None
    mm_cons = [[], []]

    b_front = {}
    last_nm = [None]
    last_r = [None]

    def front_B(c):
        t = c // 4
        s2 = c % 2
        pA = pK if s2 == 0 else pV
        tok_x, xsl = load_x(NCH + c, xs[c * 128:(c + 1) * 128, :])

        def f_attn(e, c=c, pA=pA):
            ins = None
            for dc in range(8):
                lhs = Pq[dc][:, c * 128:(c + 1) * 128]
                st, sp = dc == 0, dc == 7
                e.matmul(pA[:, 0:512], lhsT=lhs, rhs=Wo[dc][:, 0:512], start=st, stop=sp)
                ins = e.matmul(pA[:, 512:1024], lhsT=lhs, rhs=Wo[dc][:, 512:1024],
                               start=st, stop=sp)
            return ins
        qdeps = [tok_qevac[(t, dc)] for dc in range(8)]
        w_bank = lz_tmp[c - 2] if c >= 2 else pA_free[s2]
        tok_amm = P.step("te", f_attn, waits=[tok_w1, w_bank] + qdeps, inc="pt")
        b_front[c] = (tok_x, xsl, tok_amm)

    def back_B(c):
        s2 = c % 2
        pA = pK if s2 == 0 else pV
        tok_x, xsl, tok_amm = b_front[c]

        tok_tmp = P.step("sc", lambda e, s2=s2, c=c, pA=pA: e.activation(
            out=tmpB[s2][:], in_=pA.ap(), func=AF.Identity, scale=rqcols[:, c:c + 1]),
            waits=[tok_amm, tok_rqc] + list(tmpB_cons[s2]), inc="psc")
        tmpB_cons[s2][:] = []
        lz_tmp[c].tok = tok_tmp
        tok_tb = P.step("gp", lambda e, s2=s2: e.tensor_add(
            tmpB[s2][:], tmpB[s2][:], bor[:]),
            waits=[tok_tmp, tok_consts], inc="pg")
        tok_x1 = P.step("gp", lambda e, s2=s2, xsl=xsl: e.tensor_add(
            xch[xsl][:], xch[xsl][:], tmpB[s2][:]),
            waits=[tok_tb, tok_x], inc="pg")
        tmpB_cons[s2].append(tok_x1)

        tok_x1w = P.step("sc", lambda e, c=c, xsl=xsl: e.dma_start(
            out=x1scr[c * 128:(c + 1) * 128, :], in_=xch[xsl][:]),
            waits=[tok_x1], inc=f"dw{xsl}", add=16)
        xslot_cons[xsl].append(tok_x1w)

        tok_bn0 = P.step("ve", lambda e, s2=s2, xsl=xsl: e.bn_stats(
            out=bnraw[s2][:, 0:6], in_=xch[xsl][:, 0:512]),
            waits=[tok_x1] + list(stat_cons[s2]), inc="pv")
        tok_bn = P.step("ve", lambda e, s2=s2, xsl=xsl: e.bn_stats(
            out=bnraw[s2][:, 6:12], in_=xch[xsl][:, 512:1024]),
            waits=[tok_x1], inc="pv")
        stat_cons[s2][:] = []
        tok_ag = P.step("ve", lambda e, s2=s2: e.bn_aggr(out=mv[s2][:], in_=bnraw[s2][:]),
                        waits=[tok_bn0, tok_bn], inc="pv")
        tok_sq = P.step("sc", lambda e, s2=s2: e.activation(
            out=sqt[s2][:], in_=mv[s2][:, 1:2], func=AF.Sqrt, bias=c_eps[:]),
            waits=[tok_ag], inc="psc")
        tok_r = P.step("ve", lambda e, s2=s2, c=c: e.reciprocal(
            out=r2all[:, c:c + 1], in_=sqt[s2][:]),
            waits=[tok_sq], inc="pv")
        tok_nm = P.step("ve", lambda e, s2=s2, c=c: e.scalar_tensor_tensor(
            out=nm2all[:, c:c + 1], in0=mv[s2][:, 0:1], scalar=-1.0,
            in1=r2all[:, c:c + 1], op0=ALU.mult, op1=ALU.mult),
            waits=[tok_r, tok_ag], inc="pv")
        last_nm[0] = tok_nm
        last_r[0] = tok_r
        tok_c1 = P.step("ve", lambda e, s2=s2, xsl=xsl: e.reduce_max(
            out=t1c[s2][:], in_=xch[xsl][:], axis=AX.X),
            waits=[tok_x1] + list(mm_cons[s2]), inc="pv")
        mm_cons[s2][:] = []
        tok_c2 = P.step("ve", lambda e, s2=s2, xsl=xsl: e.tensor_reduce(
            out=t2c[s2][:], in_=xch[xsl][:], axis=AX.X, op=ALU.min),
            waits=[tok_x1], inc="pv")
        xslot_cons[xsl] += [tok_c1, tok_c2, tok_bn]
        tok_d1 = P.step("ve", lambda e, s2=s2, c=c: e.scalar_tensor_tensor(
            out=t1c[s2][:], in0=t1c[s2][:], scalar=mv[s2][:, 0:1], in1=r2all[:, c:c + 1],
            op0=ALU.subtract, op1=ALU.mult),
            waits=[tok_c1, tok_r], inc="pv")
        tok_d2 = P.step("ve", lambda e, s2=s2, c=c: e.scalar_tensor_tensor(
            out=t2c[s2][:], in0=t2c[s2][:], scalar=mv[s2][:, 0:1], in1=r2all[:, c:c + 1],
            op0=ALU.subtract, op1=ALU.mult),
            waits=[tok_c2, tok_r], inc="pv")
        stat_cons[s2][:] = [tok_nm, tok_d1, tok_d2, tok_r]
        tok_mx = P.step("ve", lambda e, s2=s2, c=c: e.scalar_tensor_tensor(
            out=mxcols[:, c:c + 1], in0=t2c[s2][:], scalar=-1.0, in1=t1c[s2][:],
            op0=ALU.mult, op1=ALU.max),
            waits=[tok_d1, tok_d2], inc="pv")
        mm_cons[s2][:] = [tok_mx]
        return tok_mx

    tok_mx = None
    for c in range(NCH):
        front_B(c)
        if c >= 2:
            tok_mx = back_B(c - 2)
    tok_mx = back_B(NCH - 2)
    tok_mx = back_B(NCH - 1)

    tok_m1 = P.step("ve", lambda e: e.reduce_max(
        out=mxfin[:], in_=mxcols[:, 0:NCH], axis=AX.X),
        waits=[tok_mx], inc="pv")
    tok_m2 = P.step("te", lambda e: e.transpose(pQ[0][0:1, 0:128], mxfin[:], idf[:]),
                    waits=[tok_m1, pQ_free[0]], inc="pt")
    tok_m3 = P.step("sc", lambda e: e.copy(out=mxrow[:], in_=pQ[0][0:1, 0:128]),
                    waits=[tok_m2], inc="psc")
    pQ_free[0] = tok_m3
    tok_m4 = P.step("ve", lambda e: e.reduce_max(
        out=mxstrip[0:1, 0:1], in_=mxrow[:], axis=AX.X),
        waits=[tok_m3, tok_msets], inc="pv")
    tok_m5 = P.step("sy", lambda e: e.dma_start(out=cc_mx_i.ap(), in_=mxstrip[:]),
                    waits=[tok_m4], inc="ds", add=16)
    tok_cc2 = P.step("gp", lambda e: e.collective_compute(
        "AllReduce", ALU.max, replica_groups=[list(range(8))],
        ins=[cc_mx_i.ap().opt()], outs=[cc_mx_o.ap().opt()]),
        waits=[tok_m5], inc="ccs")
    tok_m6 = P.step("sy", lambda e: e.dma_start(out=mxstrip[:], in_=cc_mx_o.ap()),
                    waits=[tok_cc2], inc="ds", add=16)

    tok_s1 = P.step("sc", lambda e: e.activation(
        out=strips[0:1, 2:3], in_=mxstrip[0:1, 0:1], func=AF.Identity, scale=c_i127[:]),
        waits=[tok_m6], inc="psc")
    tok_s2 = P.step("ve", lambda e: e.reciprocal(
        out=strips[0:1, 5:6], in_=strips[0:1, 2:3]),
        waits=[tok_s1], inc="pv")
    tok_s3 = P.step("ve", lambda e: e.scalar_tensor_tensor(
        out=strips[0:1, 3:4], in0=mxstrip[0:1, 0:1], scalar=c_i127[0:1, :],
        in1=strips[0:1, 0:1], op0=ALU.mult, op1=ALU.mult),
        waits=[tok_m6, tok_wsc, tok_msets], inc="pv")
    tok_s5 = P.step("ve", lambda e: e.reciprocal(
        out=strips[0:1, 4:5], in_=strips[0:1, 3:4]),
        waits=[tok_s3, tok_s2], inc="pv")
    tok_bp = P.step("ve", lambda e: e.tensor_single_scalar(
        out=bprime[:], in_=bfs[:], scalar=strips[0:1, 4:5], op=ALU.mult),
        waits=[tok_s5, tok_consts], inc="pv")
    tok_b1 = P.step("te", lambda e: e.matmul(
        pQ[0][:, 0:1], lhsT=onesrf[:], rhs=strips[0:1, 5:6], start=True, stop=True),
        waits=[tok_s2, pQ_free[0]], inc="pt")
    tok_b2 = P.step("te", lambda e: e.matmul(
        pQ[0][:, 1:2], lhsT=onesrf[:], rhs=strips[0:1, 3:4], start=True, stop=True),
        waits=[tok_s3], inc="pt")

    def f_bc(e):
        e.copy(out=invc[:], in_=pQ[0][:, 0:1])
        return e.copy(out=cc_c[:], in_=pQ[0][:, 1:2])
    tok_bc = P.step("sc", f_bc, waits=[tok_b1, tok_b2], inc="psc")
    pQ_free[0] = tok_bc

    tok_sc2 = P.step("ve", lambda e: e.tensor_scalar(
        out=sc2all[:], in0=r2all[:], scalar1=invc[:], scalar2=None, op0=ALU.mult),
        waits=[tok_bc, last_nm[0], last_r[0]], inc="pv")
    tok_bc2 = P.step("ve", lambda e: e.tensor_scalar(
        out=bc2all[:], in0=nm2all[:], scalar1=invc[:], scalar2=None, op0=ALU.mult),
        waits=[tok_bc, last_nm[0]], inc="pv")

    u_cons = [[rq_dma_toks[-1], tok_kvw], []]
    qn_cons = [list(qn_guard[0]) + list(qn_guard[1]),
               list(qn_guard[0]) + list(qn_guard[1])]
    i_cons = [[[], []], [[], []]]
    qiT_cons = [[], []]
    scf_consC = [list(stage_cons[0]), list(stage_cons[1])]
    pF_free = list(pA_free)
    lz_ctr = [Lz() for _ in range(NCH)]
    dwtot = [P.cnt["dw0"], P.cnt["dw1"], P.cnt["dw2"]]

    c_front = {}

    def front_C(c):
        s2 = c % 2
        sl = (2 * NCH + c) % 3
        tok_xl = P.step("sy", lambda e, c=c, sl=sl: e.dma_start(
            out=xch[sl][:], in_=x1scr[c * 128:(c + 1) * 128, :]),
            waits=list(xslot_cons[sl]) + [(f"dw{sl}", dwtot[sl])], inc=f"dx{sl}", add=16)
        xslot_cons[sl][:] = []

        tok_qn_parts = []
        prev_qn = [lz_ctr[c - 2]] if c >= 2 else list(qn_cons[s2])
        for h in range(2):
            tok_u = P.step("sc", lambda e, h=h, c=c, s2=s2, sl=sl: e.activation(
                out=ibuf[s2][h][:], in_=xch[sl][:, h * 512:(h + 1) * 512],
                func=AF.Identity,
                scale=sc2all[:, c:c + 1], bias=bc2all[:, c:c + 1]),
                waits=[tok_xl, tok_sc2, tok_bc2] + list(u_cons[h]) + list(i_cons[s2][h]),
                inc="psc")
            u_cons[h][:] = []
            i_cons[s2][h][:] = []
            xslot_cons[sl].append(tok_u)
            tok_qn = P.step("ve", lambda e, h=h, s2=s2: e.tensor_scalar(
                out=qnch[s2][:, h * 512:(h + 1) * 512], in0=ibuf[s2][h][:],
                scalar1=127.0, scalar2=-128.0, op0=ALU.min, op1=ALU.max),
                waits=[tok_u] + prev_qn, inc="pv")
            i_cons[s2][h].append(tok_qn)
            tok_qn_parts.append(tok_qn)
        c_front[c] = (tok_xl, sl, tok_qn_parts)

    def back_C(c):
        s2 = c % 2
        pF = pK if s2 == 0 else pV
        tok_xl, sl, tok_qn_parts = c_front[c]

        def f_qitr(e, s2=s2):
            ins = None
            for dc in range(8):
                ins = e.transpose(pT_bf[:, dc * 128:(dc + 1) * 128],
                                  qnch[s2][:, dc * 128:(dc + 1) * 128], idbf[:])
            return ins
        tok_tr = P.step("te", f_qitr, waits=tok_qn_parts + [pT_free[0]], inc="pt")
        lz_ctr[c].tok = tok_tr
        tok_trev = P.step("sc", lambda e, s2=s2: e.copy(
            out=qiT[s2][:], in_=pT_bf.ap()),
            waits=[tok_tr] + list(qiT_cons[s2]), inc="psc")
        qiT_cons[s2][:] = []
        pT_free[0] = tok_trev

        def f_ffn(e, s2=s2, pF=pF):
            for dc in range(8):
                lhs = qiT[s2][:, dc * 128:(dc + 1) * 128]
                st = dc == 0
                e.matmul(pF[:, 0:512], lhsT=lhs, rhs=qwT[:, dc * D:dc * D + 512],
                         start=st, stop=False)
                e.matmul(pF[:, 512:1024], lhsT=lhs, rhs=qwT[:, dc * D + 512:(dc + 1) * D],
                         start=st, stop=False)
            e.matmul(pF[:, 0:512], lhsT=onesrb[:], rhs=bprime[0:1, 0:512],
                     start=False, stop=True)
            return e.matmul(pF[:, 512:1024], lhsT=onesrb[:], rhs=bprime[0:1, 512:1024],
                            start=False, stop=True)
        tok_fmm = P.step("te", f_ffn,
                         waits=[tok_trev, qwT_toks[-1], tok_bp, pF_free[s2], tok_consts],
                         inc="pt")
        qiT_cons[s2].append(tok_fmm)

        tok_x2 = P.step("ve", lambda e, s2=s2, sl=sl, pF=pF: e.scalar_tensor_tensor(
            out=scf[s2][:], in0=pF.ap(), scalar=cc_c[:], in1=xch[sl][:],
            op0=ALU.mult, op1=ALU.add),
            waits=[tok_fmm, tok_xl, tok_bc] + list(scf_consC[s2]), inc="pv")
        scf_consC[s2][:] = []
        pF_free[s2] = tok_x2
        xslot_cons[sl].append(tok_x2)

        tok_ow = P.step("sc", lambda e, c=c, s2=s2: e.dma_start(
            out=out[c * 128:(c + 1) * 128, :], in_=scf[s2][:]),
            waits=[tok_x2], inc=f"do{s2}", add=16)
        scf_consC[s2].append(tok_ow)

    for c in range(NCH):
        front_C(c)
        if c >= 2:
            back_C(c - 2)
    back_C(NCH - 2)
    back_C(NCH - 1)

    P.step("sy", lambda e: e.nop(),
           waits=[("do0", P.cnt["do0"]), ("do1", P.cnt["do1"])])

    P.run()
    es.close()
    return nc


def make_inputs(x, wqkv, bqkv, wout, bout, wffn, bffn, tpc):
    import ml_dtypes
    bf = ml_dtypes.bfloat16
    tokens = x.shape[0]
    bqc = np.ascontiguousarray(bqkv[:D].reshape(8, 128).T)
    consts = {
        "wqkv": wqkv, "bqkv": bqkv, "wout": wout, "bout": bout,
        "wffn": wffn, "bffn": bffn, "bqc": bqc,
        "idbf": np.eye(128, dtype=bf),
        "idf": np.eye(128, dtype=np.float32),
        "onescb": np.ones((128, 1), dtype=bf),
        "onesrb": np.ones((1, 128), dtype=bf),
        "onescf": np.ones((128, 1), dtype=np.float32),
        "onesrf": np.ones((1, 128), dtype=np.float32),
    }
    in_maps = []
    for i in range(NCORE):
        m = dict(consts)
        m["xs"] = np.ascontiguousarray(x[i * tpc:(i + 1) * tpc])
        in_maps.append(m)
    return in_maps


def kernel(L0_attn_x, L0_attn_Wqkv, L0_attn_bqkv, L0_attn_Wout, L0_attn_bout,
           L0_ffn_weight, L0_ffn_bias):
    x = np.asarray(L0_attn_x, dtype=np.float32)
    B, TT, Dm = x.shape
    tokens = B * TT
    tpc = tokens // NCORE
    nc = build(tpc)
    in_maps = make_inputs(
        np.ascontiguousarray(x.reshape(tokens, Dm)),
        np.ascontiguousarray(np.asarray(L0_attn_Wqkv, dtype=np.float32)),
        np.ascontiguousarray(np.asarray(L0_attn_bqkv, dtype=np.float32)),
        np.ascontiguousarray(np.asarray(L0_attn_Wout, dtype=np.float32)),
        np.ascontiguousarray(np.asarray(L0_attn_bout, dtype=np.float32)),
        np.ascontiguousarray(np.asarray(L0_ffn_weight, dtype=np.float32)),
        np.ascontiguousarray(np.asarray(L0_ffn_bias, dtype=np.float32)),
        tpc)
    res = run_bass_kernel_spmd(nc, in_maps, core_ids=list(range(NCORE)))
    outs = [np.asarray(res.results[i]["out"]) for i in range(NCORE)]
    return np.concatenate(outs, axis=0).reshape(B, TT, Dm)


# version 23 (final)
# speedup vs baseline: 1.4513x; 1.4513x over previous
import numpy as np
from contextlib import ExitStack

import concourse.bass as bass
import concourse.mybir as mybir
from concourse.bass_utils import run_bass_kernel_spmd

F32 = mybir.dt.float32
BF16 = mybir.dt.bfloat16
I32 = mybir.dt.int32
U8 = mybir.dt.uint8
AF = mybir.ActivationFunctionType
ALU = mybir.AluOpType
AX = mybir.AxisListType

D = 1024
NCORE = 8


class Lz:
    __slots__ = ("tok",)

    def __init__(self):
        self.tok = None


class Prog:
    ENGS = ("sy", "gp", "ve", "sc", "te")

    def __init__(self, nc):
        self.nc = nc
        self.q = {e: [] for e in self.ENGS}
        self.cnt = {}
        self.sems = {}

    def sem(self, name):
        h = self.nc.alloc_semaphore(name=name)
        self.sems[name] = h
        self.cnt[name] = 0

    def step(self, eng, fn, waits=(), inc=None, add=1):
        tok = None
        if inc is not None:
            self.cnt[inc] += add
            tok = (inc, self.cnt[inc])
        ws = [w for w in waits if w is not None]
        self.q[eng].append((ws, fn, inc, add))
        return tok

    def replay(self, name, eng):
        seen = {}
        for ws, fn, inc, add in self.q[name]:
            for w in ws:
                if isinstance(w, Lz):
                    assert w.tok is not None, "unresolved lazy token"
                    s, v = w.tok
                else:
                    s, v = w
                if v > seen.get(s, 0):
                    eng.wait_ge(self.sems[s], v)
                    seen[s] = v
            ins = fn(eng)
            if inc is not None:
                ins.then_inc(self.sems[inc], add)

    def run(self):
        with self.nc.Block() as block:
            @block.sync
            def _(e):
                self.replay("sy", e)

            @block.gpsimd
            def _(e):
                self.replay("gp", e)

            @block.vector
            def _(e):
                self.replay("ve", e)

            @block.scalar
            def _(e):
                self.replay("sc", e)

            @block.tensor
            def _(e):
                self.replay("te", e)


def build(T):
    NT = T // 512
    NCH = T // 128
    nc = bass.Bass("TRN2", target_bir_lowering=False, debug=False,
                   dynamic_dma_scratch_size=512)

    xs = nc.dram_tensor("xs", [T, D], F32, kind="ExternalInput").ap()
    wqkv = nc.dram_tensor("wqkv", [D, 3 * D], F32, kind="ExternalInput").ap()
    bqkv = nc.dram_tensor("bqkv", [3 * D], F32, kind="ExternalInput").ap()
    wout = nc.dram_tensor("wout", [D, D], F32, kind="ExternalInput").ap()
    bout = nc.dram_tensor("bout", [D], F32, kind="ExternalInput").ap()
    wffn = nc.dram_tensor("wffn", [D, D], F32, kind="ExternalInput").ap()
    bffn = nc.dram_tensor("bffn", [D], F32, kind="ExternalInput").ap()
    bqc_in = nc.dram_tensor("bqc", [128, 8], F32, kind="ExternalInput").ap()
    idbf_in = nc.dram_tensor("idbf", [128, 128], BF16, kind="ExternalInput").ap()
    idf_in = nc.dram_tensor("idf", [128, 128], F32, kind="ExternalInput").ap()
    onescb_in = nc.dram_tensor("onescb", [128, 1], BF16, kind="ExternalInput").ap()
    onesrb_in = nc.dram_tensor("onesrb", [1, 128], BF16, kind="ExternalInput").ap()
    onescf_in = nc.dram_tensor("onescf", [128, 1], F32, kind="ExternalInput").ap()
    onesrf_in = nc.dram_tensor("onesrf", [1, 128], F32, kind="ExternalInput").ap()
    out = nc.dram_tensor("out", [T, D], F32, kind="ExternalOutput").ap()
    x1scr = nc.dram_tensor("x1scr", [T, D], F32).ap()
    rqscr = nc.dram_tensor("rqscr", [T], F32).ap()
    cc_kv_i = nc.dram_tensor("cc_kv_i", [8, 128], F32)
    cc_kv_o = nc.dram_tensor("cc_kv_o", [8, 128], F32)
    cc_mx_i = nc.dram_tensor("cc_mx_i", [1, 16], F32)
    cc_mx_o = nc.dram_tensor("cc_mx_o", [1, 16], F32)

    es = ExitStack()
    sbt = lambda n, s, d: es.enter_context(nc.sbuf_tensor(n, s, d))
    pst = lambda n, s, d: es.enter_context(nc.psum_tensor(n, s, d))

    Pq = [sbt(f"pq{i}", [128, T], BF16) for i in range(8)]
    arena = sbt("arena", [128, 49152], U8)
    abase = nc.lookup_mloc(arena).addr
    Wq = [nc.alloc_sbuf_tensor_at(f"wq{i}", [128, 3 * D], BF16, offset=abase + i * 6144)
          for i in range(8)]
    qw = [nc.alloc_sbuf_tensor_at(f"qwq{i}", [128, D], BF16, offset=abase + i * 2048)
          for i in range(8)]
    qwT = nc.alloc_sbuf_tensor_at("qwT", [128, 8 * D], BF16, offset=abase + 16384)
    tmpB = [nc.alloc_sbuf_tensor_at(f"tmpb{i}", [128, D], F32, offset=abase + 32768 + i * 4096)
            for i in range(4)]
    Wo = [sbt(f"wo{i}", [128, D], BF16) for i in range(8)]
    h1T = [sbt(f"h1t{i}", [128, 8 * 512], BF16) for i in range(2)]
    xch = [sbt(f"xch{i}", [128, D], F32) for i in range(3)]
    h1ch = [sbt(f"h1ch{i}", [128, D], BF16) for i in range(4)]
    kch = [sbt(f"kch{i}", [128, D], BF16) for i in range(2)]
    vch = [sbt(f"vch{i}", [128, D], BF16) for i in range(2)]
    kvp = [sbt(f"kvp{i}", [128, D], BF16) for i in range(2)]
    qsqs = [sbt(f"qsqs{i}", [128, 512], BF16) for i in range(2)]
    scf = [sbt(f"scf{i}", [128, D], F32) for i in range(2)]
    ubuf = [sbt("ubuf0", [128, 512], F32)]
    ibuf = [[sbt(f"ibuf{i}{h}", [128, 512], I32) for h in range(2)] for i in range(2)]
    qnch = [sbt(f"qnch{i}", [128, D], BF16) for i in range(2)]
    qiT = [sbt(f"qit{i}", [128, D], BF16) for i in range(2)]
    bkr = sbt("bkr", [128, D], BF16)
    bvr = sbt("bvr", [128, D], BF16)
    bor = sbt("bor", [128, D], F32)
    bqc = sbt("bqcs", [128, 8], F32)
    bfs = sbt("bfs", [1, D], F32)
    bprime = sbt("bprime", [1, D], BF16)
    idbf = sbt("idbfs", [128, 128], BF16)
    idf = sbt("idfs", [128, 128], F32)
    onescb = sbt("onescbs", [128, 1], BF16)
    onesrb = sbt("onesrbs", [1, 128], BF16)
    onescf = sbt("onescfs", [128, 1], F32)
    onesrf = sbt("onesrfs", [1, 128], F32)
    bnraw = [sbt(f"bnraw{i}", [128, 12], F32) for i in range(4)]
    mv = [sbt(f"mv{i}", [128, 2], F32) for i in range(4)]
    rcol = [sbt(f"rcol{i}", [128, 1], F32) for i in range(4)]
    nmcol = [sbt(f"nmcol{i}", [128, 1], F32) for i in range(4)]
    kss = [sbt(f"kss{i}", [128, 1], F32) for i in range(2)]
    rkcol = [sbt(f"rkcol{i}", [128, 1], F32) for i in range(2)]
    r2all = sbt("r2all", [128, NCH], F32)
    nm2all = sbt("nm2all", [128, NCH], F32)
    sc2all = sbt("sc2all", [128, NCH], F32)
    bc2all = sbt("bc2all", [128, NCH], F32)
    sqt = [sbt(f"sqt{i}", [128, 1], F32) for i in range(4)]
    ksq = [sbt(f"ksq{i}", [128, 1], F32) for i in range(2)]
    rqrows = sbt("rqrows", [32, 128], F32)
    rqrec = nc.alloc_sbuf_tensor_at("rqrec", [32, 128], F32,
                                    offset=nc.lookup_mloc(ubuf[0]).addr)
    rqcols = sbt("rqcols", [128, 32], F32)
    c1all = sbt("c1all", [128, NCH], F32)
    c2all = sbt("c2all", [128, NCH], F32)
    normrow = nc.alloc_sbuf_tensor_at("normrow", [1, T], BF16,
                                      offset=nc.lookup_mloc(h1T[0]).addr)
    nstage = nc.alloc_sbuf_tensor_at("nstage", [1, T], F32,
                                     offset=nc.lookup_mloc(h1T[0]).addr)
    borow = nc.alloc_sbuf_tensor_at("borow", [1, D], BF16,
                                    offset=nc.lookup_mloc(h1T[1]).addr)
    kvrows = sbt("kvrows", [8, 128], F32)
    kvcols = sbt("kvcols", [128, 8], F32)
    mxfin = sbt("mxfin", [128, 1], F32)
    mxrow = sbt("mxrow", [1, 128], F32)
    mxstrip = sbt("mxstrip", [1, 16], F32)
    wsums = sbt("wsums", [128, 8], F32)
    wsum1 = sbt("wsum1", [128, 1], F32)
    strips = sbt("strips", [1, 16], F32)
    thrc = sbt("thrc", [128, 1], F32)
    nthrc = sbt("nthrc", [128, 1], F32)
    invc = sbt("invc", [128, 1], F32)
    cc_c = sbt("ccc", [128, 1], F32)
    c_eps = sbt("ceps", [128, 1], F32)

    pT = pst("p_tr", [128, 512], F32)
    pQ = [pst(f"p_q{i}", [128, 512], F32) for i in range(2)]
    pK = pst("p_k", [128, 1024], F32)
    pV = pst("p_v", [128, 1024], F32)
    pR = pst("p_r", [128, 512], F32)
    pT_bf = pT.bitcast(BF16)

    P = Prog(nc)
    for s in ("dx0", "dx1", "dx2", "dc", "dt0", "dt1", "dw0", "dw1", "dw2",
              "ds", "do0", "do1", "ccs", "pv", "psc", "pt", "pg", "dg"):
        P.sem(s)

    xslot_cons = [[], [], []]

    def load_x(idx, src_ap):
        s = idx % 3
        tok = P.step("sy", lambda e, a=src_ap, s=s: e.dma_start(out=xch[s][:], in_=a),
                     waits=list(xslot_cons[s]), inc=f"dx{s}", add=16)
        xslot_cons[s] = []
        return tok, s

    const_srcs = [
        (idbf, idbf_in), (idf, idf_in), (onescb, onescb_in), (onesrb, onesrb_in),
        (onescf, onescf_in), (onesrf, onesrf_in), (bqc, bqc_in),
    ]
    for dst, src in const_srcs:
        P.step("sy", lambda e, d=dst, s=src: e.dma_start(out=d[:], in_=s),
               inc="dc", add=16)
    P.step("sy", lambda e: e.dma_start(out=bor[:], in_=bout.partition_broadcast(128)),
           inc="dc", add=16)
    tok_consts = P.step("sy", lambda e: e.dma_start(
        out=bfs[:], in_=bffn.rearrange("(a b) -> a b", a=1)), inc="dc", add=16)

    def f_msets(e):
        e.memset(c_eps[:], 1e-5)
        e.memset(strips[0:1, 8:9], 1.0 / 127.0)
        e.memset(strips[0:1, 6:7], 1.0 / float(D * D))
        e.memset(strips[0:1, 7:8], 0.5)
        return e.memset(mxstrip[:], 0.0)
    tok_msets = P.step("ve", f_msets, inc="pv")
    for g in ("ve", "sc", "te"):
        P.step(g, lambda e: e.nop(), waits=[tok_consts, tok_msets])

    preload_x = {}
    for c in range(3):
        preload_x[c] = load_x(c, xs[c * 128:(c + 1) * 128, :])

    stage_cons = [[], []]
    wf_abs_toks = []
    tok_bk0 = P.step("sy", lambda e: e.dma_start(
        out=scf[0][:], in_=bqkv[D:2 * D].partition_broadcast(128)), inc="dt0", add=16)
    tok_bk1 = P.step("ve", lambda e: e.tensor_copy(out=bkr[:], in_=scf[0][:]),
                     waits=[tok_bk0], inc="pv")
    stage_cons[0].append(tok_bk1)
    tok_bv0 = P.step("sy", lambda e: e.dma_start(
        out=scf[1][:], in_=bqkv[2 * D:3 * D].partition_broadcast(128)), inc="dt1", add=16)
    tok_bv1 = P.step("ve", lambda e: e.tensor_copy(out=bvr[:], in_=scf[1][:]),
                     waits=[tok_bv0], inc="pv")
    stage_cons[1].append(tok_bv1)
    for dc in range(8):
        P.step("gp", lambda e, dc=dc: e.dma_start(
            out=Wq[dc][:, :], in_=wqkv[dc * 128:(dc + 1) * 128, :]),
            inc="dg", add=16)
    tok_wq_kv = ("dg", 16 * 16)
    tok_wq_all = ("dg", 16 * 16)
    for dc in range(8):
        P.step("gp", lambda e, dc=dc: e.dma_start(
            out=Wo[dc][:, :], in_=wout[dc * 128:(dc + 1) * 128, :]),
            inc="dg", add=16)
    tok_wo_all = ("dg", 16 * 16)

    h1T_cons = [[], []]
    h1_cons = [[], []]
    k_cons = [[], []]
    v_cons = [[], []]
    kvp_cons = [[], []]
    stat_cons = [[], []]
    kstat_cons = [[], []]
    pQ_free = [None, None]
    pK_free = [None]
    pV_free = [None]
    tok_qevac = {}
    tok_h1T_tile = [None] * NT
    tok_kvacc = None
    qsq_cons = [[], []]
    rq_dma_toks = []
    pend_kvacc = []
    kv_fin = [None]

    trev_toks = [None] * NCH
    s4cons = [[], [], [], []]
    h1_cons4 = [[], [], [], []]
    pTa_free = [None]
    pTb_free = [None]

    def front_LN(c):
        s4 = c % 4
        if c < 3:
            tok_x, xsl = preload_x[c]
        else:
            tok_x, xsl = load_x(c, xs[c * 128:(c + 1) * 128, :])
        tok_bn0 = P.step("ve", lambda e, s4=s4, xsl=xsl: e.bn_stats(
            out=bnraw[s4][:, 0:6], in_=xch[xsl][:, 0:512]),
            waits=[tok_x] + list(s4cons[s4]), inc="pv")
        tok_bn = P.step("ve", lambda e, s4=s4, xsl=xsl: e.bn_stats(
            out=bnraw[s4][:, 6:12], in_=xch[xsl][:, 512:1024]),
            waits=[tok_x], inc="pv")
        s4cons[s4][:] = []
        xslot_cons[xsl] += [tok_bn0, tok_bn]
        tok_ag = P.step("ve", lambda e, s4=s4: e.bn_aggr(out=mv[s4][:], in_=bnraw[s4][:]),
                        waits=[tok_bn0, tok_bn], inc="pv")
        tok_sq = P.step("sc", lambda e, s4=s4: e.activation(
            out=sqt[s4][:], in_=mv[s4][:, 1:2], func=AF.Sqrt, bias=c_eps[:]),
            waits=[tok_ag], inc="psc")
        tok_r = P.step("ve", lambda e, s4=s4: e.reciprocal(
            out=rcol[s4][:], in_=sqt[s4][:]),
            waits=[tok_sq], inc="pv")
        tok_nm = P.step("ve", lambda e, s4=s4: e.scalar_tensor_tensor(
            out=nmcol[s4][:], in0=mv[s4][:, 0:1], scalar=-1.0, in1=rcol[s4][:],
            op0=ALU.mult, op1=ALU.mult),
            waits=[tok_r, tok_ag], inc="pv")
        tok_h1 = P.step("sc", lambda e, s4=s4, xsl=xsl: e.activation(
            out=h1ch[s4][:], in_=xch[xsl][:], func=AF.Identity,
            scale=rcol[s4][:], bias=nmcol[s4][:]),
            waits=[tok_nm, tok_r] + list(h1_cons4[s4]), inc="psc")
        h1_cons4[s4][:] = []
        xslot_cons[xsl].append(tok_h1)
        s4cons[s4][:] = [tok_h1, tok_nm, tok_r]
        trev_toks[c] = tok_h1

    def front_TR(c):
        t, j = c // 4, c % 4
        s4 = c % 4
        hb = t % 2
        tok_h1 = trev_toks[c]

        def f_tr(e, s4=s4):
            ins = None
            for dc in range(8):
                ins = e.transpose(pT_bf[:, dc * 128:(dc + 1) * 128],
                                  h1ch[s4][:, dc * 128:(dc + 1) * 128], idbf[:])
            return ins
        tok_tr = P.step("te", f_tr, waits=[tok_h1, pTa_free[0], pTb_free[0]], inc="pt")

        def f_trev(e, hb=hb, j=j):
            dst = h1T[hb].ap().rearrange("p (dc t) -> p dc t", dc=8)[:, :, j * 128:(j + 1) * 128]
            return e.copy(out=dst, in_=pT_bf.ap().rearrange("p (dc t) -> p dc t", dc=8))
        tok_trev = P.step("sc", f_trev,
                          waits=[tok_tr] + (list(h1T_cons[hb]) if j == 0 else []),
                          inc="psc")
        if j == 0:
            h1T_cons[hb][:] = []
        pTa_free[0] = tok_trev
        pTb_free[0] = tok_trev
        h1_cons4[s4] += [tok_trev]
        trev_toks[c] = (tok_trev, tok_trev)
        if j == 3:
            tok_h1T_tile[t] = tok_trev

    def back_A(c):
        t, j = c // 4, c % 4
        s2 = c % 2
        hb = t % 2
        tok_eva, tok_evb = trev_toks[c]
        if len(pend_kvacc) >= 2:
            kv_fin[0] = pend_kvacc.pop(0)()

        def f_k(e, hb=hb, j=j):
            ins = None
            for dc in range(8):
                lhs = h1T[hb][:, dc * 512 + j * 128: dc * 512 + (j + 1) * 128]
                st, sp = dc == 0, dc == 7
                e.matmul(pK[:, 0:512], lhsT=lhs, rhs=Wq[dc][:, D:D + 512], start=st, stop=sp)
                ins = e.matmul(pK[:, 512:1024], lhsT=lhs, rhs=Wq[dc][:, D + 512:2 * D], start=st, stop=sp)
            return ins
        tok_kmm = P.step("te", f_k, waits=[tok_eva, tok_evb, tok_wq_kv, pK_free[0]],
                         inc="pt")

        def f_v(e, hb=hb, j=j):
            ins = None
            for dc in range(8):
                lhs = h1T[hb][:, dc * 512 + j * 128: dc * 512 + (j + 1) * 128]
                st, sp = dc == 0, dc == 7
                e.matmul(pV[:, 0:512], lhsT=lhs, rhs=Wq[dc][:, 2 * D:2 * D + 512], start=st, stop=sp)
                ins = e.matmul(pV[:, 512:1024], lhsT=lhs, rhs=Wq[dc][:, 2 * D + 512:3 * D], start=st, stop=sp)
            return ins
        tok_vmm = P.step("te", f_v, waits=[pV_free[0]], inc="pt")
        h1T_cons[hb].append(tok_vmm)

        tok_ke = P.step("ve", lambda e, s2=s2: e.scalar_tensor_tensor(
            out=kch[s2][:], in0=pK.ap(), scalar=1.0, in1=bkr[:],
            op0=ALU.bypass, op1=ALU.add),
            waits=[tok_kmm, tok_bk1] + list(k_cons[s2]), inc="pv")
        k_cons[s2][:] = []
        pK_free[0] = tok_ke
        tok_vev = P.step("ve", lambda e, s2=s2: e.scalar_tensor_tensor(
            out=vch[s2][:], in0=pV.ap(), scalar=1.0, in1=bvr[:],
            op0=ALU.bypass, op1=ALU.add),
            waits=[tok_vmm, tok_bv1] + list(v_cons[s2]), inc="pv")
        v_cons[s2][:] = []
        pV_free[0] = tok_vev

        tok_ks = P.step("sc", lambda e, s2=s2: e.activation(
            out=kvp[s2][:], in_=kch[s2][:], func=AF.Square, accum_out=kss[s2][:]),
            waits=[tok_ke] + list(kvp_cons[s2]) + list(kstat_cons[s2]), inc="psc")
        kvp_cons[s2][:] = []
        tok_rk0 = P.step("sc", lambda e, s2=s2: e.activation(
            out=ksq[s2][:], in_=kss[s2][:], func=AF.Sqrt),
            waits=[tok_ks] + list(kstat_cons[s2]), inc="psc")
        tok_rk = P.step("ve", lambda e, s2=s2: e.reciprocal(
            out=rkcol[s2][:], in_=ksq[s2][:]),
            waits=[tok_rk0], inc="pv")
        tok_kvp = P.step("ve", lambda e, s2=s2: e.scalar_tensor_tensor(
            out=kvp[s2][:], in0=kch[s2][:], scalar=rkcol[s2][:], in1=vch[s2][:],
            op0=ALU.mult, op1=ALU.mult),
            waits=[tok_rk, tok_vev, tok_ks], inc="pv")
        k_cons[s2].append(tok_kvp)
        v_cons[s2].append(tok_kvp)
        kstat_cons[s2][:] = [tok_kvp]

        def q_kvacc(s2=s2, c=c, tok_kvp=tok_kvp):
            def f_kvacc(e):
                e.matmul(pR[0:1, :], lhsT=onescb[:], rhs=kvp[s2][:, 0:512],
                         start=(c == 0), stop=(c == NCH - 1))
                return e.matmul(pR[32:33, :], lhsT=onescb[:], rhs=kvp[s2][:, 512:1024],
                                start=(c == 0), stop=(c == NCH - 1))
            tok = P.step("te", f_kvacc, waits=[tok_kvp, tok_consts], inc="pt")
            kvp_cons[s2].append(tok)
            return tok
        pend_kvacc.append(q_kvacc)

        if j == 3:
            if t == NT - 1:
                while pend_kvacc:
                    kv_fin[0] = pend_kvacc.pop(0)()
            for ec in range(8):
                def f_q(e, t=t, ec=ec, hb=hb):
                    ins = None
                    for dc in range(8):
                        ins = e.matmul(pQ[ec % 2][:],
                                       lhsT=Wq[dc][:, ec * 128:(ec + 1) * 128],
                                       rhs=h1T[hb][:, dc * 512:(dc + 1) * 512],
                                       start=(dc == 0), stop=(dc == 7))
                    return ins
                tok_q = P.step("te", f_q,
                               waits=[tok_h1T_tile[t], tok_wq_all, pQ_free[ec % 2]],
                               inc="pt")
                h1T_cons[hb].append(tok_q)
                tok_ev = P.step("sc", lambda e, t=t, ec=ec: e.activation(
                    out=Pq[ec][:, t * 512:(t + 1) * 512], in_=pQ[ec % 2].ap(),
                    func=AF.Identity, bias=bqc[:, ec:ec + 1]),
                    waits=[tok_q, tok_consts], inc="psc")
                tok_qevac[(t, ec)] = tok_ev
                pQ_free[ec % 2] = tok_ev
            tok_sq_mm = None
            for ec in range(8):
                tok_sq = P.step("ve", lambda e, t=t, ec=ec: e.tensor_mul(
                    qsqs[ec % 2][:], Pq[ec][:, t * 512:(t + 1) * 512],
                    Pq[ec][:, t * 512:(t + 1) * 512]),
                    waits=[tok_qevac[(t, ec)]] + list(qsq_cons[ec % 2]), inc="pv")
                qsq_cons[ec % 2][:] = []
                tok_sq_mm = P.step("te", lambda e, ec=ec: e.matmul(
                    pR[64:65, :], lhsT=onescb[:], rhs=qsqs[ec % 2][:],
                    start=(ec == 0), stop=(ec == 7)),
                    waits=[tok_sq], inc="pt")
                qsq_cons[ec % 2].append(tok_sq_mm)
            tok_qse = P.step("sc", lambda e: e.copy(
                out=ubuf[0][0:1, :], in_=pR[64:65, :]),
                waits=[tok_sq_mm] + (rq_dma_toks[-1:] if t >= 1 else []), inc="psc")
            tok_rqw = P.step("sy", lambda e, t=t: e.dma_start(
                out=rqscr[t * 512:(t + 1) * 512].rearrange("(a b) -> a b", a=1),
                in_=ubuf[0][0:1, :]),
                waits=[tok_qse], inc="ds", add=16)
            rq_dma_toks.append(tok_rqw)

    def wf1_pair(oc):
        sl = oc % 2
        tok_l = P.step("sy" if sl == 0 else "sc",
            lambda e, oc=oc, sl=sl: e.dma_start(
                out=scf[sl][:], in_=wffn[oc * 128:(oc + 1) * 128, :]),
            waits=list(stage_cons[sl]), inc=f"dt{sl}", add=16)
        stage_cons[sl][:] = []
        tok_a = P.step("ve", lambda e, oc=oc, sl=sl: e.reduce_sum(
            out=wsums[:, oc:oc + 1], in_=scf[sl][:], axis=AX.X,
            apply_absolute_value=True),
            waits=[tok_l], inc="pv")
        stage_cons[sl].append(tok_a)
        wf_abs_toks.append(tok_a)

    for t in range(NT):
        for j in range(4):
            if t >= 1:
                front_TR(4 * (t - 1) + j)
            front_LN(4 * t + j)
            if t >= 2:
                back_A(4 * (t - 2) + j)
        if NCH >= 32 and t >= NT - 4:
            wf1_pair(2 * (t - (NT - 4)))
            wf1_pair(2 * (t - (NT - 4)) + 1)
    for j in range(4):
        front_TR(4 * (NT - 1) + j)
        if NT >= 2:
            back_A(4 * (NT - 2) + j)
    for j in range(4):
        back_A(4 * (NT - 1) + j)
    if NCH < 32:
        for oc in range(8):
            wf1_pair(oc)
    if NCH < 32:
        for oc in range(8):
            wf1_pair(oc)

    while pend_kvacc:
        kv_fin[0] = pend_kvacc.pop(0)()
    tok_kvacc = kv_fin[0]

    tok_ws1 = P.step("ve", lambda e: e.reduce_sum(out=wsum1[:], in_=wsums[:], axis=AX.X),
                     waits=[wf_abs_toks[-1]], inc="pv")
    tok_wsmm = P.step("te", lambda e: e.matmul(
        pQ[0][0:1, 4:5], lhsT=onescf[:], rhs=wsum1[:], start=True, stop=True),
        waits=[tok_ws1, pQ_free[0]], inc="pt")
    tok_wsc = P.step("sc", lambda e: e.activation(
        out=strips[0:1, 0:1], in_=pQ[0][0:1, 4:5], func=AF.Identity, scale=strips[0:1, 6:7]),
        waits=[tok_wsmm, tok_msets], inc="psc")
    tok_thr = P.step("sc", lambda e: e.activation(
        out=strips[0:1, 1:2], in_=strips[0:1, 0:1], func=AF.Identity, scale=strips[0:1, 7:8]),
        waits=[tok_wsc], inc="psc")
    tok_thrmm = P.step("te", lambda e: e.matmul(
        pQ[0][:, 0:1], lhsT=onesrf[:], rhs=strips[0:1, 1:2], start=True, stop=True),
        waits=[tok_thr, pQ_free[0]], inc="pt")
    tok_thrc = P.step("sc", lambda e: e.copy(out=thrc[:], in_=pQ[0][:, 0:1]),
                      waits=[tok_thrmm], inc="psc")
    pQ_free[0] = tok_thrc
    tok_nthr = P.step("ve", lambda e: e.tensor_scalar(
        out=nthrc[:], in0=thrc[:], scalar1=-1.0, scalar2=None, op0=ALU.mult),
        waits=[tok_thrc], inc="pv")

    qw_toks = []
    qn_guard = [[], []]
    for oc in range(8):
        sl = oc % 2
        tok_l = P.step("sy" if sl == 0 else "sc",
            lambda e, oc=oc, sl=sl: e.dma_start(
                out=scf[sl][:], in_=wffn[oc * 128:(oc + 1) * 128, :]),
            waits=list(stage_cons[sl]), inc=f"dt{sl}", add=16)
        stage_cons[sl] = []
        tok_ta = P.step("ve", lambda e, sl=sl: e.tensor_single_scalar(
            out=qnch[0][:], in_=scf[sl][:], scalar=thrc[:], op=ALU.is_gt),
            waits=[tok_l, tok_nthr] + list(qn_guard[0]), inc="pv")
        tok_tb2 = P.step("ve", lambda e, sl=sl: e.tensor_single_scalar(
            out=qnch[1][:], in_=scf[sl][:], scalar=nthrc[:], op=ALU.is_lt),
            waits=[tok_l, tok_nthr] + list(qn_guard[1]), inc="pv")
        tok_q = P.step("ve", lambda e, oc=oc: e.tensor_sub(
            qw[oc][:], qnch[0][:], qnch[1][:]),
            waits=[tok_ta, tok_tb2], inc="pv")
        qn_guard[0] = [tok_q]
        qn_guard[1] = [tok_q]
        stage_cons[sl].append(tok_q)
        qw_toks.append(tok_q)
    qwT_toks = []
    for oc in range(8):
        def f_qtr(e, oc=oc):
            ins = None
            for dc in range(8):
                ins = e.transpose(pT_bf[:, dc * 128:(dc + 1) * 128],
                                  qw[oc][:, dc * 128:(dc + 1) * 128], idbf[:])
            return ins
        tok_tr = P.step("te", f_qtr, waits=[qw_toks[oc], pTa_free[0], pTb_free[0]], inc="pt")

        def f_qte(e, oc=oc):
            dst = qwT.ap().rearrange("p (dc o) -> p dc o", dc=8)[:, :, oc * 128:(oc + 1) * 128]
            return e.copy(out=dst, in_=pT_bf.ap().rearrange("p (dc t) -> p dc t", dc=8))
        tok_te = P.step("sc", f_qte, waits=[tok_tr], inc="psc")
        pTa_free[0] = tok_te
        pTb_free[0] = tok_te
        qwT_toks.append(tok_te)

    def f_kvev(e):
        e.copy(out=ubuf[0][32:33, :], in_=pR[0:1, :])
        return e.copy(out=ubuf[0][64:65, :], in_=pR[32:33, :])
    tok_kve = P.step("sc", f_kvev, waits=[tok_kvacc], inc="psc")
    P.step("sy", lambda e: e.dma_start(
        out=cc_kv_i.ap()[0:4, :], in_=ubuf[0][32:33, :]),
        waits=[tok_kve], inc="ds", add=16)
    tok_kvw = P.step("sy", lambda e: e.dma_start(
        out=cc_kv_i.ap()[4:8, :], in_=ubuf[0][64:65, :]),
        waits=[tok_kve], inc="ds", add=16)
    tok_cc1 = P.step("gp", lambda e: e.collective_compute(
        "AllReduce", ALU.add, replica_groups=[[0, 1, 2, 3], [4, 5, 6, 7]],
        ins=[cc_kv_i.ap().opt()], outs=[cc_kv_o.ap().opt()]),
        waits=[tok_kvw], inc="ccs")
    tok_kvr = P.step("sy", lambda e: e.dma_start(out=kvrows[:], in_=cc_kv_o.ap()),
                     waits=[tok_cc1], inc="ds", add=16)
    tok_kvtr = P.step("te", lambda e: e.transpose(pQ[1][:, 0:8], kvrows[:], idf[0:8, 0:8]),
                      waits=[tok_kvr, pQ_free[1]], inc="pt")
    tok_kvc = P.step("sc", lambda e: e.copy(out=kvcols[:], in_=pQ[1][:, 0:8]),
                     waits=[tok_kvtr], inc="psc")
    pQ_free[1] = tok_kvc

    def f_w1(e):
        ins = None
        for dc in range(8):
            ins = e.tensor_scalar(out=Wo[dc][:], in0=Wo[dc][:],
                                  scalar1=kvcols[:, dc:dc + 1], scalar2=None,
                                  op0=ALU.mult)
        return ins
    tok_w1 = P.step("ve", f_w1, waits=[tok_kvc, tok_wo_all], inc="pv")

    tok_rqr = P.step("sy", lambda e: e.dma_start(
        out=rqrows[0:NCH, :], in_=rqscr.rearrange("(a b) -> a b", b=128)),
        waits=[("ds", P.cnt["ds"])], inc="ds", add=16)
    tok_rqs = P.step("sc", lambda e: e.activation(
        out=rqrows[0:NCH, :], in_=rqrows[0:NCH, :], func=AF.Sqrt),
        waits=[tok_rqr], inc="psc")
    tok_rqv = P.step("ve", lambda e: e.reciprocal(
        out=rqrec[0:NCH, :], in_=rqrows[0:NCH, :]),
        waits=[tok_rqs], inc="pv")
    tok_rqt = P.step("te", lambda e: e.transpose(
        pQ[0][:, 0:NCH], rqrec[0:NCH, :], idf[0:NCH, 0:NCH]),
        waits=[tok_rqv, pQ_free[0]], inc="pt")
    tok_nst = P.step("sy", lambda e: e.dma_start(
        out=nstage[:], in_=rqrows[0:NCH, :]),
        waits=[tok_rqs] + list(h1T_cons[1]), inc="ds", add=16)
    tok_nrow = P.step("sc", lambda e: e.copy(out=normrow[:], in_=nstage[:]),
                      waits=[tok_nst], inc="psc")
    tok_borow = P.step("ve", lambda e: e.tensor_copy(
        out=borow[:], in_=bor[0:1, :]),
        waits=[tok_consts, tok_nrow], inc="pv")
    tok_rqc = P.step("sc", lambda e: e.copy(out=rqcols[:, 0:NCH], in_=pQ[0][:, 0:NCH]),
                     waits=[tok_rqt], inc="psc")
    pQ_free[0] = tok_rqc

    tmpB_cons = [[], [], [], []]
    lz_tmp = [Lz() for _ in range(NCH)]
    pA_free = [pK_free[0], pV_free[0]]
    tok_mx = None

    b_front = {}
    last_nm = [None]
    last_r = [None]

    def front_B(c):
        t = c // 4
        s2 = c % 2
        pA = pK if s2 == 0 else pV
        tok_x, xsl = load_x(NCH + c, xs[c * 128:(c + 1) * 128, :])

        def f_attn(e, c=c, pA=pA):
            for dc in range(8):
                lhs = Pq[dc][:, c * 128:(c + 1) * 128]
                st = dc == 0
                e.matmul(pA[:, 0:512], lhsT=lhs, rhs=Wo[dc][:, 0:512], start=st, stop=False)
                e.matmul(pA[:, 512:1024], lhsT=lhs, rhs=Wo[dc][:, 512:1024],
                         start=st, stop=False)
            nr = normrow[0:1, c * 128:(c + 1) * 128]
            e.matmul(pA[:, 0:512], lhsT=nr, rhs=borow[0:1, 0:512], start=False, stop=True)
            return e.matmul(pA[:, 512:1024], lhsT=nr, rhs=borow[0:1, 512:1024],
                            start=False, stop=True)
        qdeps = [tok_qevac[(t, dc)] for dc in range(8)]
        w_bank = lz_tmp[c - 2] if c >= 2 else pA_free[s2]
        tok_amm = P.step("te", f_attn,
                         waits=[tok_w1, w_bank, tok_nrow, tok_borow] + qdeps, inc="pt")
        b_front[c] = (tok_x, xsl, tok_amm)

    def back_B(c):
        s2 = c % 2
        s4 = c % 4
        pA = pK if s2 == 0 else pV
        tok_x, xsl, tok_amm = b_front[c]

        tok_tmp = P.step("sc", lambda e, s4=s4, c=c, pA=pA: e.activation(
            out=tmpB[s4][:], in_=pA.ap(), func=AF.Identity, scale=rqcols[:, c:c + 1]),
            waits=[tok_amm, tok_rqc] + list(tmpB_cons[s4]), inc="psc")
        tmpB_cons[s4][:] = []
        lz_tmp[c].tok = tok_tmp
        tok_x1 = P.step("gp", lambda e, s4=s4, xsl=xsl: e.tensor_add(
            xch[xsl][:], xch[xsl][:], tmpB[s4][:]),
            waits=[tok_tmp, tok_x], inc="pg")
        tmpB_cons[s4].append(tok_x1)

        tok_x1w = P.step("sc", lambda e, c=c, xsl=xsl: e.dma_start(
            out=x1scr[c * 128:(c + 1) * 128, :], in_=xch[xsl][:]),
            waits=[tok_x1], inc=f"dw{xsl}", add=16)
        xslot_cons[xsl].append(tok_x1w)

        tok_bn0 = P.step("ve", lambda e, s2=s2, xsl=xsl: e.bn_stats(
            out=bnraw[s2][:, 0:6], in_=xch[xsl][:, 0:512]),
            waits=[tok_x1] + list(stat_cons[s2]), inc="pv")
        tok_bn = P.step("ve", lambda e, s2=s2, xsl=xsl: e.bn_stats(
            out=bnraw[s2][:, 6:12], in_=xch[xsl][:, 512:1024]),
            waits=[tok_x1], inc="pv")
        stat_cons[s2][:] = []
        tok_ag = P.step("ve", lambda e, s2=s2: e.bn_aggr(out=mv[s2][:], in_=bnraw[s2][:]),
                        waits=[tok_bn0, tok_bn], inc="pv")
        tok_sq = P.step("sc", lambda e, s2=s2: e.activation(
            out=sqt[s2][:], in_=mv[s2][:, 1:2], func=AF.Sqrt, bias=c_eps[:]),
            waits=[tok_ag], inc="psc")
        tok_r = P.step("ve", lambda e, s2=s2, c=c: e.reciprocal(
            out=r2all[:, c:c + 1], in_=sqt[s2][:]),
            waits=[tok_sq], inc="pv")
        tok_nm = P.step("ve", lambda e, s2=s2, c=c: e.scalar_tensor_tensor(
            out=nm2all[:, c:c + 1], in0=mv[s2][:, 0:1], scalar=-1.0,
            in1=r2all[:, c:c + 1], op0=ALU.mult, op1=ALU.mult),
            waits=[tok_r, tok_ag], inc="pv")
        last_nm[0] = tok_nm
        last_r[0] = tok_r
        tok_c1 = P.step("ve", lambda e, c=c, xsl=xsl: e.reduce_max(
            out=c1all[:, c:c + 1], in_=xch[xsl][:], axis=AX.X),
            waits=[tok_x1], inc="pv")
        tok_c2 = P.step("ve", lambda e, c=c, xsl=xsl: e.tensor_reduce(
            out=c2all[:, c:c + 1], in_=xch[xsl][:], axis=AX.X, op=ALU.min),
            waits=[tok_x1], inc="pv")
        xslot_cons[xsl] += [tok_c1, tok_c2, tok_bn]
        stat_cons[s2][:] = [tok_nm, tok_r]
        return tok_c2

    tok_mx = None
    for c in range(NCH):
        front_B(c)
        if c >= 2:
            tok_mx = back_B(c - 2)
    tok_mx = back_B(NCH - 2)
    tok_mx = back_B(NCH - 1)

    tok_b1 = P.step("ve", lambda e: e.tensor_tensor(
        out=c1all[:, 0:NCH], in0=c1all[:, 0:NCH], in1=r2all[:, 0:NCH], op=ALU.mult),
        waits=[tok_mx, last_r[0]], inc="pv")
    tok_b2 = P.step("ve", lambda e: e.tensor_tensor(
        out=c1all[:, 0:NCH], in0=c1all[:, 0:NCH], in1=nm2all[:, 0:NCH], op=ALU.add),
        waits=[tok_b1, last_nm[0]], inc="pv")
    tok_b3 = P.step("ve", lambda e: e.tensor_tensor(
        out=c2all[:, 0:NCH], in0=c2all[:, 0:NCH], in1=r2all[:, 0:NCH], op=ALU.mult),
        waits=[tok_mx], inc="pv")
    tok_b4 = P.step("ve", lambda e: e.tensor_tensor(
        out=c2all[:, 0:NCH], in0=c2all[:, 0:NCH], in1=nm2all[:, 0:NCH], op=ALU.add),
        waits=[tok_b3], inc="pv")
    tok_b5 = P.step("ve", lambda e: e.scalar_tensor_tensor(
        out=c1all[:, 0:NCH], in0=c2all[:, 0:NCH], scalar=-1.0, in1=c1all[:, 0:NCH],
        op0=ALU.mult, op1=ALU.max),
        waits=[tok_b2, tok_b4], inc="pv")

    tok_m1 = P.step("ve", lambda e: e.reduce_max(
        out=mxfin[:], in_=c1all[:, 0:NCH], axis=AX.X),
        waits=[tok_b5], inc="pv")
    tok_m2 = P.step("te", lambda e: e.transpose(pQ[0][0:1, 0:128], mxfin[:], idf[:]),
                    waits=[tok_m1, pQ_free[0]], inc="pt")
    tok_m3 = P.step("sc", lambda e: e.copy(out=mxrow[:], in_=pQ[0][0:1, 0:128]),
                    waits=[tok_m2], inc="psc")
    pQ_free[0] = tok_m3
    tok_m4 = P.step("ve", lambda e: e.reduce_max(
        out=mxstrip[0:1, 0:1], in_=mxrow[:], axis=AX.X),
        waits=[tok_m3, tok_msets], inc="pv")
    tok_m5 = P.step("sy", lambda e: e.dma_start(out=cc_mx_i.ap(), in_=mxstrip[:]),
                    waits=[tok_m4], inc="ds", add=16)
    tok_cc2 = P.step("gp", lambda e: e.collective_compute(
        "AllReduce", ALU.max, replica_groups=[list(range(8))],
        ins=[cc_mx_i.ap().opt()], outs=[cc_mx_o.ap().opt()]),
        waits=[tok_m5], inc="ccs")
    tok_m6 = P.step("sy", lambda e: e.dma_start(out=mxstrip[:], in_=cc_mx_o.ap()),
                    waits=[tok_cc2], inc="ds", add=16)

    tok_s1 = P.step("sc", lambda e: e.activation(
        out=strips[0:1, 2:3], in_=mxstrip[0:1, 0:1], func=AF.Identity, scale=strips[0:1, 8:9]),
        waits=[tok_m6], inc="psc")
    tok_s2 = P.step("ve", lambda e: e.reciprocal(
        out=strips[0:1, 5:6], in_=strips[0:1, 2:3]),
        waits=[tok_s1], inc="pv")
    tok_s3 = P.step("ve", lambda e: e.scalar_tensor_tensor(
        out=strips[0:1, 3:4], in0=mxstrip[0:1, 0:1], scalar=strips[0:1, 8:9],
        in1=strips[0:1, 0:1], op0=ALU.mult, op1=ALU.mult),
        waits=[tok_m6, tok_wsc, tok_msets], inc="pv")
    tok_s5 = P.step("ve", lambda e: e.reciprocal(
        out=strips[0:1, 4:5], in_=strips[0:1, 3:4]),
        waits=[tok_s3, tok_s2], inc="pv")
    tok_bp = P.step("ve", lambda e: e.tensor_single_scalar(
        out=bprime[:], in_=bfs[:], scalar=strips[0:1, 4:5], op=ALU.mult),
        waits=[tok_s5, tok_consts], inc="pv")
    tok_b1 = P.step("te", lambda e: e.matmul(
        pQ[0][:, 0:1], lhsT=onesrf[:], rhs=strips[0:1, 5:6], start=True, stop=True),
        waits=[tok_s2, pQ_free[0]], inc="pt")
    tok_b2 = P.step("te", lambda e: e.matmul(
        pQ[0][:, 1:2], lhsT=onesrf[:], rhs=strips[0:1, 3:4], start=True, stop=True),
        waits=[tok_s3], inc="pt")

    def f_bc(e):
        e.copy(out=invc[:], in_=pQ[0][:, 0:1])
        return e.copy(out=cc_c[:], in_=pQ[0][:, 1:2])
    tok_bc = P.step("sc", f_bc, waits=[tok_b1, tok_b2], inc="psc")
    pQ_free[0] = tok_bc

    tok_sc2 = P.step("ve", lambda e: e.tensor_scalar(
        out=sc2all[:], in0=r2all[:], scalar1=invc[:], scalar2=None, op0=ALU.mult),
        waits=[tok_bc, last_nm[0], last_r[0]], inc="pv")
    tok_bc2 = P.step("ve", lambda e: e.tensor_scalar(
        out=bc2all[:], in0=nm2all[:], scalar1=invc[:], scalar2=None, op0=ALU.mult),
        waits=[tok_bc, last_nm[0]], inc="pv")

    u_cons = [[rq_dma_toks[-1], tok_kvw], []]
    qn_cons = [list(qn_guard[0]) + list(qn_guard[1]),
               list(qn_guard[0]) + list(qn_guard[1])]
    i_cons = [[[], []], [[], []]]
    qiT_cons = [[], []]
    scf_consC = [list(stage_cons[0]), list(stage_cons[1])]
    pF_free = list(pA_free)
    lz_ctr = [Lz() for _ in range(NCH)]
    c_trev = [None] * NCH
    pC_free = [None, None]
    dwtot = [P.cnt["dw0"], P.cnt["dw1"], P.cnt["dw2"]]

    c_front = {}

    def front_C(c):
        s2 = c % 2
        sl = (2 * NCH + c) % 3
        tok_xl = P.step("sy", lambda e, c=c, sl=sl: e.dma_start(
            out=xch[sl][:], in_=x1scr[c * 128:(c + 1) * 128, :]),
            waits=list(xslot_cons[sl]) + [(f"dw{sl}", dwtot[sl])], inc=f"dx{sl}", add=16)
        xslot_cons[sl][:] = []

        tok_qn_parts = []
        prev_qn = [lz_ctr[c - 2]] if c >= 2 else list(qn_cons[s2])
        for h in range(2):
            tok_u = P.step("ve", lambda e, h=h, c=c, s2=s2, sl=sl: e.tensor_scalar(
                out=ibuf[s2][h][:], in0=xch[sl][:, h * 512:(h + 1) * 512],
                scalar1=sc2all[:, c:c + 1], scalar2=bc2all[:, c:c + 1],
                op0=ALU.mult, op1=ALU.add),
                waits=[tok_xl, tok_sc2, tok_bc2] + list(u_cons[h]) + list(i_cons[s2][h]),
                inc="pv")
            u_cons[h][:] = []
            i_cons[s2][h][:] = []
            xslot_cons[sl].append(tok_u)
            tok_qn = P.step("ve", lambda e, h=h, s2=s2: e.tensor_scalar(
                out=qnch[s2][:, h * 512:(h + 1) * 512], in0=ibuf[s2][h][:],
                scalar1=127.0, scalar2=-128.0, op0=ALU.min, op1=ALU.max),
                waits=[tok_u] + prev_qn, inc="pv")
            i_cons[s2][h].append(tok_qn)
            tok_qn_parts.append(tok_qn)
        c_front[c] = (tok_xl, sl, tok_qn_parts)

    pC_bf = [pT_bf, pQ[0].bitcast(BF16)]

    def back_C_tr(c):
        s2 = c % 2
        tok_xl, sl, tok_qn_parts = c_front[c]

        def f_qitr(e, s2=s2, bk=c % 2):
            ins = None
            for dc in range(8):
                ins = e.transpose(pC_bf[bk][:, dc * 128:(dc + 1) * 128],
                                  qnch[s2][:, dc * 128:(dc + 1) * 128], idbf[:])
            return ins
        bank_guard = pQ_free[0] if c == 1 else (pC_free[c % 2] if c >= 2 else None)
        extra = [pTa_free[0], pTb_free[0]] if c == 0 else []
        tok_tr = P.step("te", f_qitr, waits=tok_qn_parts + [bank_guard] + extra, inc="pt")
        lz_ctr[c].tok = tok_tr
        tok_trev = P.step("sc", lambda e, s2=s2, bk=c % 2: e.copy(
            out=qiT[s2][:], in_=pC_bf[bk].ap()),
            waits=[tok_tr] + list(qiT_cons[s2]), inc="psc")
        qiT_cons[s2][:] = []
        pC_free[c % 2] = tok_trev
        c_trev[c] = tok_trev

    def back_C_mm(c):
        s2 = c % 2
        pF = pK if s2 == 0 else pV
        tok_xl, sl, tok_qn_parts = c_front[c]
        tok_trev = c_trev[c]

        def f_ffn(e, s2=s2, pF=pF):
            for dc in range(8):
                lhs = qiT[s2][:, dc * 128:(dc + 1) * 128]
                st = dc == 0
                e.matmul(pF[:, 0:512], lhsT=lhs, rhs=qwT[:, dc * D:dc * D + 512],
                         start=st, stop=False)
                e.matmul(pF[:, 512:1024], lhsT=lhs, rhs=qwT[:, dc * D + 512:(dc + 1) * D],
                         start=st, stop=False)
            e.matmul(pF[:, 0:512], lhsT=onesrb[:], rhs=bprime[0:1, 0:512],
                     start=False, stop=True)
            return e.matmul(pF[:, 512:1024], lhsT=onesrb[:], rhs=bprime[0:1, 512:1024],
                            start=False, stop=True)
        tok_fmm = P.step("te", f_ffn,
                         waits=[tok_trev, qwT_toks[-1], tok_bp, pF_free[s2], tok_consts],
                         inc="pt")
        qiT_cons[s2].append(tok_fmm)

        tok_x2 = P.step("ve", lambda e, s2=s2, sl=sl, pF=pF: e.scalar_tensor_tensor(
            out=scf[s2][:], in0=pF.ap(), scalar=cc_c[:], in1=xch[sl][:],
            op0=ALU.mult, op1=ALU.add),
            waits=[tok_fmm, tok_xl, tok_bc] + list(scf_consC[s2]), inc="pv")
        scf_consC[s2][:] = []
        pF_free[s2] = tok_x2
        xslot_cons[sl].append(tok_x2)

        tok_ow = P.step("sc", lambda e, c=c, s2=s2: e.dma_start(
            out=out[c * 128:(c + 1) * 128, :], in_=scf[s2][:]),
            waits=[tok_x2], inc=f"do{s2}", add=16)
        scf_consC[s2].append(tok_ow)

    for c in range(NCH):
        front_C(c)
        if c >= 1:
            back_C_tr(c - 1)
        if c >= 2:
            back_C_mm(c - 2)
    back_C_tr(NCH - 1)
    back_C_mm(NCH - 2)
    back_C_mm(NCH - 1)

    P.step("sy", lambda e: e.nop(),
           waits=[("do0", P.cnt["do0"]), ("do1", P.cnt["do1"])])

    P.run()
    es.close()
    return nc


def make_inputs(x, wqkv, bqkv, wout, bout, wffn, bffn, tpc):
    import ml_dtypes
    bf = ml_dtypes.bfloat16
    tokens = x.shape[0]
    bqc = np.ascontiguousarray(bqkv[:D].reshape(8, 128).T)
    consts = {
        "wqkv": wqkv, "bqkv": bqkv, "wout": wout, "bout": bout,
        "wffn": wffn, "bffn": bffn, "bqc": bqc,
        "idbf": np.eye(128, dtype=bf),
        "idf": np.eye(128, dtype=np.float32),
        "onescb": np.ones((128, 1), dtype=bf),
        "onesrb": np.ones((1, 128), dtype=bf),
        "onescf": np.ones((128, 1), dtype=np.float32),
        "onesrf": np.ones((1, 128), dtype=np.float32),
    }
    in_maps = []
    for i in range(NCORE):
        m = dict(consts)
        m["xs"] = np.ascontiguousarray(x[i * tpc:(i + 1) * tpc])
        in_maps.append(m)
    return in_maps


def kernel(L0_attn_x, L0_attn_Wqkv, L0_attn_bqkv, L0_attn_Wout, L0_attn_bout,
           L0_ffn_weight, L0_ffn_bias):
    x = np.asarray(L0_attn_x, dtype=np.float32)
    B, TT, Dm = x.shape
    tokens = B * TT
    tpc = tokens // NCORE
    nc = build(tpc)
    in_maps = make_inputs(
        np.ascontiguousarray(x.reshape(tokens, Dm)),
        np.ascontiguousarray(np.asarray(L0_attn_Wqkv, dtype=np.float32)),
        np.ascontiguousarray(np.asarray(L0_attn_bqkv, dtype=np.float32)),
        np.ascontiguousarray(np.asarray(L0_attn_Wout, dtype=np.float32)),
        np.ascontiguousarray(np.asarray(L0_attn_bout, dtype=np.float32)),
        np.ascontiguousarray(np.asarray(L0_ffn_weight, dtype=np.float32)),
        np.ascontiguousarray(np.asarray(L0_ffn_bias, dtype=np.float32)),
        tpc)
    res = run_bass_kernel_spmd(nc, in_maps, core_ids=list(range(NCORE)))
    outs = [np.asarray(res.results[i]["out"]) for i in range(NCORE)]
    return np.concatenate(outs, axis=0).reshape(B, TT, Dm)
